# version 07 (final)
# speedup vs baseline: 1.2011x; 1.0118x over previous
import numpy as np
import ml_dtypes
from contextlib import ExitStack

import concourse.bass as bass
import concourse.mybir as mybir
from concourse.bass_utils import run_bass_kernel_spmd

F32 = mybir.dt.float32
BF16 = mybir.dt.bfloat16
I32 = mybir.dt.int32
AF = mybir.ActivationFunctionType
ALU = mybir.AluOpType
AX = mybir.AxisListType

NCORE = 8
T = 4096
D = 1024
NST = 16
RK = 64
NC8 = D // 128
NT = T // 128


class Prog:
    def __init__(self):
        self.nc = bass.Bass("TRN2", target_bir_lowering=False, detect_race_conditions=False)
        self.stack = ExitStack()
        self.sems = {}

    def sem(self, name):
        h = self.stack.enter_context(self.nc.semaphore(name))
        self.sems[name] = [h, 0]

    def inc(self, name, instr, k=1):
        s = self.sems[name]
        instr.then_inc(s[0], k)
        s[1] += k
        return s[1]

    def val(self, name):
        return self.sems[name][1]

    def wait(self, eng, name, v=None):
        s = self.sems[name]
        v = s[1] if v is None else v
        if v > 0:
            eng.wait_ge(s[0], v)

    def barrier(self, engines=None):
        nc = self.nc
        engines = engines if engines is not None else [
            nc.tensor, nc.vector, nc.scalar, nc.gpsimd, nc.sync]
        for e in engines:
            for nm, (h, n) in self.sems.items():
                if n:
                    e.wait_ge(h, n)

    def sb(self, name, shape, dt):
        return self.stack.enter_context(self.nc.sbuf_tensor(name, list(shape), dt))

    def ps(self, name, shape, dt=F32):
        return self.stack.enter_context(self.nc.psum_tensor(name, list(shape), dt))

    def din(self, name, shape, dt):
        return self.nc.dram_tensor(name, list(shape), dt, kind="ExternalInput")

    def dout(self, name, shape, dt):
        return self.nc.dram_tensor(name, list(shape), dt, kind="ExternalOutput")

    def dscratch(self, name, shape, dt, shared=False):
        return self.nc.dram_tensor(name, list(shape), dt, kind="Internal",
                                   addr_space="Shared" if shared else "Local")


def emit_ln(p, tag, x_tile, out_tile, sq_scratch):
    nc = p.nc
    Dm = x_tile.shape[1]
    mv = p.sb(f"lnmv{tag}", (128, 4), F32)
    rs = p.sb(f"lnrs{tag}", (128, 1), F32)
    p.wait(nc.scalar, "v")
    p.inc("a", nc.scalar.activation(sq_scratch[:, 0:Dm], x_tile, AF.Square))
    nc.vector.tensor_reduce(mv[:, 0:1], x_tile, AX.X, ALU.add)
    p.wait(nc.vector, "a")
    p.inc("v", nc.vector.tensor_reduce(mv[:, 1:2], sq_scratch[:, 0:Dm], AX.X, ALU.add))
    p.wait(nc.vector, "v")
    p.inc("v", nc.vector.tensor_scalar_mul(mv[:, 0:2], mv[:, 0:2], 1.0 / Dm))
    p.wait(nc.vector, "v")
    p.inc("v", nc.vector.tensor_tensor(mv[:, 2:3], mv[:, 0:1], mv[:, 0:1], ALU.mult))
    p.wait(nc.vector, "v")
    p.inc("v", nc.vector.tensor_tensor(mv[:, 2:3], mv[:, 1:2], mv[:, 2:3], ALU.subtract))
    p.wait(nc.scalar, "v")
    p.inc("a", nc.scalar.activation(mv[:, 3:4], mv[:, 2:3], AF.Sqrt, bias=p.cst_eps))
    p.wait(nc.vector, "a")
    p.inc("v", nc.vector.reciprocal(rs[:], mv[:, 3:4]))
    p.wait(nc.vector, "v")
    p.inc("v", nc.vector.tensor_scalar(out_tile, x_tile, mv[:, 0:1], rs[:],
                                       ALU.subtract, ALU.mult))



def build_L1():
    p = Prog()
    nc = p.nc
    SPAN = 512
    NSPAN = T // SPAN
    JT = SPAN // 128

    x_in = p.din("x", (T, D), F32)
    w_in = p.din("w_in", (D, D), F32)
    w_x = p.din("w_x", (D, 96), F32)
    w_dt = p.din("w_dt", (RK, D), F32)
    b_dt = p.din("b_dt", (D,), F32)
    A_log = p.din("A_log", (D, NST), F32)
    Dp = p.din("Dp", (D,), F32)
    ident_in = p.din("identin", (128, 128), BF16)

    s1p = p.dout("s1p", (T, D), F32)
    hend_o = p.dout("hend", (128, 128), F32)
    dtlt_o = p.dout("dtlt", (RK, T), BF16)
    c_o = p.dout("ctr", (NST, T), BF16)
    cdbg_o = p.dout("cdbg", (8, 128, 128), F32)

    reps_d = p.dscratch("repsd", (2 * NST, SPAN), BF16)

    with p.stack:
        for s in ("t", "v", "a", "ds", "dg"):
            p.sem(s)

        wa = p.sb("wa", (128, NC8 * D), BF16)
        wx = p.sb("wx", (128, NC8 * 96), BF16)
        wdt = p.sb("wdt", (RK, D), BF16)
        bdtr = p.sb("bdtr", (1, D), BF16)
        alc = p.sb("alc", (128, 128), F32)
        negA = p.sb("negA", (128, 128), F32)
        sAc = p.sb("sAc", (128, 128), F32)
        sBc = p.sb("sBc", (128, 128), F32)
        arhs = p.sb("arhs", (128, 128), BF16)
        dpc = p.sb("dpc", (128, NC8), F32)
        ident = p.sb("ident", (128, 128), BF16)
        ones_bf = p.sb("ones_bf", (1, SPAN), BF16)
        cst = p.sb("cst", (128, 2), F32)
        nc.vector.memset(cst[:, 0:1], 1e-5)
        p.inc("v", nc.vector.memset(cst[:, 1:2], -1.0))
        p.cst_eps = cst[:, 0:1]
        p.cst_neg1 = cst[:, 1:2]

        p.inc("dg", nc.gpsimd.dma_start(
            wa.ap().rearrange("p (k m) -> p k m", k=NC8),
            w_in.ap().rearrange("(k p) m -> p k m", p=128)), 16)
        p.inc("dg", nc.gpsimd.dma_start(
            wx.ap().rearrange("p (k m) -> p k m", k=NC8),
            w_x.ap().rearrange("(k p) m -> p k m", p=128)), 16)
        p.inc("dg", nc.gpsimd.dma_start(wdt[:], w_dt.ap()), 16)
        p.inc("dg", nc.gpsimd.dma_start(bdtr[:], b_dt.ap().unsqueeze(0)), 16)
        p.inc("ds", nc.sync.dma_start(
            alc.ap().rearrange("p (c n) -> p c n", c=NC8),
            A_log.ap().rearrange("(c p) n -> p c n", p=128)), 16)
        for c in range(NC8):
            p.inc("ds", nc.sync.dma_start(
                dpc[:, c:c + 1], Dp.ap()[c * 128:(c + 1) * 128].unsqueeze(-1)), 16)
        p.inc("ds", nc.sync.dma_start(ident[:], ident_in.ap()), 16)
        p.inc("v", nc.vector.memset(ones_bf[:], 1.0))
        p.barrier()
        p.inc("a", nc.scalar.activation(negA[:], alc[:], AF.Exp))
        p.wait(nc.vector, "a")
        nc.vector.tensor_scalar_mul(negA[:], negA[:], -1.0)
        nc.vector.tensor_scalar_mul(sAc[:], negA[:], 0.099)
        nc.vector.tensor_scalar_mul(sBc[:], negA[:], 0.001)
        p.inc("v", nc.vector.tensor_copy(arhs[:], negA[:]))
        p.barrier()

        xsp = p.sb("xsp", (128, JT * D), F32)
        xln = p.sb("xln", (128, JT * D), BF16)
        xlnT = p.sb("xlnT", (128, NC8 * SPAN), BF16)
        uTb = p.sb("uTb", (128, NC8 * SPAN), BF16)
        dtpreT = p.sb("dtpreT", (128, NC8 * SPAN), F32)
        dtTc = p.sb("dtTc", (128, SPAN), F32)
        zc = p.sb("zc", (128, NC8 * SPAN), BF16)
        dpu = p.sb("dpu", (128, NC8 * SPAN), BF16)
        dtl_bf = p.sb("dtl_bf", (128, JT * RK), BF16)
        dtlT = p.sb("dtlT", (RK, SPAN), BF16)
        bsp = p.sb("bsp", (128, JT * NST), F32)
        csp = p.sb("csp", (128, JT * NST), F32)
        wsp = p.sb("wsp", (128, JT * NST), F32)
        sm16 = p.sb("sm16", (128, 2 * NST), BF16)
        inpT = p.sb("inpT", (NST, SPAN), BF16)
        cT = p.sb("cT", (NST, SPAN), BF16)
        reps = p.sb("reps", (128, 2 * NST * SPAN), F32)
        lnsq = p.sb("lnsq", (128, D), F32)
        sp1 = p.sb("sp1", (128, SPAN), F32)
        sp2 = p.sb("sp2", (128, SPAN), F32)
        barA = p.sb("barA", (128, 2 * SPAN), F32)
        hbuf = p.sb("hbuf", (128, 2 * SPAN), F32)
        wtt = p.sb("wtt", (128, 2 * SPAN), BF16)
        carry = p.sb("carry", (128, 128), F32)
        ybf = p.sb("ybf", (128, SPAN), BF16)
        s1sp = p.sb("s1sp", (128, JT * D), F32)

        ppt = p.ps("ppt", (128, 2048), BF16)
        ppb = p.ps("ppb", (128, 1024), F32)
        ppp = p.ps("ppp", (128, 96), F32)
        ppw = p.ps("ppw", (128, JT * NST), F32)
        ppy = p.ps("ppy", (128, 1024), F32)

        p.inc("v", nc.vector.memset(carry[:], 0.0))
        p.barrier()

        for sp in range(NSPAN):
            p.inc("ds", nc.sync.dma_start(
                xsp.ap().rearrange("p (i m) -> p i m", i=JT),
                x_in.ap().rearrange("(i p) m -> p i m", p=128)[:, JT * sp:JT * (sp + 1), :]), 16)
            p.barrier([nc.vector, nc.scalar])
            for j in range(JT):
                emit_ln(p, f"m{sp}_{j}", xsp[:, j * D:(j + 1) * D],
                        xln[:, j * D:(j + 1) * D], lnsq)
            p.barrier([nc.tensor])

            evmk = []
            for k, (j, c) in enumerate([(j, c) for j in range(JT) for c in range(NC8)]):
                slot = (k % 2) * 1024
                if k >= 2:
                    p.wait(nc.tensor, "v", evmk[k - 2])
                p.inc("t", nc.tensor.transpose(
                    ppt[:, slot:slot + 128],
                    xln[:, j * D + c * 128: j * D + (c + 1) * 128], ident[:]))
                p.wait(nc.vector, "t")
                evmk.append(p.inc("v", nc.vector.tensor_copy(
                    xlnT[:, c * SPAN + j * 128: c * SPAN + (j + 1) * 128],
                    ppt[:, slot:slot + 128])))

            evu = []
            for m in range(NC8):
                slot = (m % 2) * 512
                if m >= 2:
                    p.wait(nc.tensor, "v", evu[m - 2])
                for k in range(NC8):
                    p.inc("t", nc.tensor.matmul(
                        ppb[:, slot:slot + 512],
                        wa[:, k * D + m * 128: k * D + (m + 1) * 128],
                        xlnT[:, k * SPAN:(k + 1) * SPAN],
                        start=(k == 0), stop=(k == NC8 - 1)))
                p.wait(nc.vector, "t")
                evu.append(p.inc("v", nc.vector.tensor_copy(
                    uTb[:, m * SPAN:(m + 1) * SPAN], ppb[:, slot:slot + 512])))

            for j in range(JT):
                p.wait(nc.tensor, "v")
                for c in range(NC8):
                    p.inc("t", nc.tensor.matmul(
                        ppp[:],
                        uTb[:, c * SPAN + j * 128: c * SPAN + (j + 1) * 128],
                        wx[:, c * 96:(c + 1) * 96],
                        start=(c == 0), stop=(c == NC8 - 1)))
                p.wait(nc.vector, "t")
                p.inc("v", nc.vector.tensor_copy(dtl_bf[:, j * RK:(j + 1) * RK], ppp[:, 0:RK]))
                p.inc("v", nc.vector.tensor_copy(bsp[:, j * NST:(j + 1) * NST], ppp[:, RK:RK + NST]))
                p.inc("v", nc.vector.tensor_copy(csp[:, j * NST:(j + 1) * NST], ppp[:, RK + NST:RK + 2 * NST]))
            p.barrier([nc.tensor])
            for j in range(JT):
                p.inc("t", nc.tensor.transpose(ppt[0:RK, 0:128],
                                               dtl_bf[:, j * RK:(j + 1) * RK], ident[:]))
                p.wait(nc.vector, "t")
                p.inc("v", nc.vector.tensor_copy(dtlT[:, j * 128:(j + 1) * 128], ppt[0:RK, 0:128]))
                p.wait(nc.tensor, "v")
            p.wait(nc.sync, "v")
            p.inc("ds", nc.sync.dma_start(dtlt_o.ap()[:, sp * SPAN:(sp + 1) * SPAN], dtlT[:]), 16)

            for c in range(NC8):
                slot = (c % 2) * 512
                if c >= 2:
                    p.wait(nc.tensor, "a")
                p.inc("t", nc.tensor.matmul(ppb[:, slot:slot + 512],
                                            wdt[:, c * 128:(c + 1) * 128], dtlT[:],
                                            start=True, stop=False))
                p.inc("t", nc.tensor.matmul(ppb[:, slot:slot + 512],
                                            bdtr[:, c * 128:(c + 1) * 128], ones_bf[:],
                                            start=False, stop=True))
                p.wait(nc.scalar, "t")
                p.wait(nc.scalar, "v")
                nc.scalar.activation(sp1[:], ppb[:, slot:slot + 512], AF.Abs)
                nc.scalar.activation(sp1[:], sp1[:], AF.Exp, scale=p.cst_neg1)
                nc.scalar.activation(sp1[:], sp1[:], AF.Ln, bias=1.0)
                p.inc("a", nc.scalar.activation(sp2[:], ppb[:, slot:slot + 512], AF.Relu))
                p.wait(nc.vector, "a")
                p.inc("v", nc.vector.tensor_tensor(
                    dtpreT[:, c * SPAN:(c + 1) * SPAN], sp1[:], sp2[:], ALU.add))

            for c in range(NC8):
                p.inc("v", nc.vector.tensor_scalar(
                    dtTc[:], dtpreT[:, c * SPAN:(c + 1) * SPAN], 0.099, 0.001,
                    ALU.mult, ALU.add))
                p.wait(nc.vector, "v")
                p.inc("v", nc.vector.tensor_tensor(
                    zc[:, c * SPAN:(c + 1) * SPAN], dtTc[:],
                    uTb[:, c * SPAN:(c + 1) * SPAN], ALU.mult))
                p.inc("v", nc.vector.tensor_scalar_mul(
                    dpu[:, c * SPAN:(c + 1) * SPAN],
                    uTb[:, c * SPAN:(c + 1) * SPAN], dpc[:, c:c + 1]))
            p.wait(nc.tensor, "v")
            for j in range(JT):
                for c in range(NC8):
                    p.inc("t", nc.tensor.matmul(
                        ppw[:, j * NST:(j + 1) * NST],
                        zc[:, c * SPAN + j * 128: c * SPAN + (j + 1) * 128],
                        arhs[:, c * NST:(c + 1) * NST],
                        start=(c == 0), stop=(c == NC8 - 1)))
            p.wait(nc.vector, "t")
            for j in range(JT):
                p.inc("v", nc.vector.tensor_tensor(
                    wsp[:, j * NST:(j + 1) * NST], ppw[:, j * NST:(j + 1) * NST],
                    bsp[:, j * NST:(j + 1) * NST], ALU.mult))
                p.wait(nc.vector, "v")
            for j in range(JT):
                p.inc("v", nc.vector.tensor_copy(sm16[:, 0:NST], wsp[:, j * NST:(j + 1) * NST]))
                p.wait(nc.vector, "v")
                p.inc("v", nc.vector.tensor_copy(sm16[:, NST:2 * NST], csp[:, j * NST:(j + 1) * NST]))
                p.wait(nc.tensor, "v")
                p.inc("t", nc.tensor.transpose(ppt[0:NST, 0:128], sm16[:, 0:NST], ident[:]))
                p.inc("t", nc.tensor.transpose(ppt[0:NST, 1024:1152], sm16[:, NST:2 * NST], ident[:]))
                p.wait(nc.vector, "t")
                p.inc("v", nc.vector.tensor_copy(inpT[:, j * 128:(j + 1) * 128], ppt[0:NST, 0:128]))
                p.inc("v", nc.vector.tensor_copy(cT[:, j * 128:(j + 1) * 128], ppt[0:NST, 1024:1152]))
                p.wait(nc.tensor, "v")
            p.wait(nc.sync, "v")
            p.inc("ds", nc.sync.dma_start(reps_d.ap()[0:NST, :], inpT[:]), 16)
            p.inc("ds", nc.sync.dma_start(reps_d.ap()[NST:2 * NST, :], cT[:]), 16)
            p.inc("ds", nc.sync.dma_start(c_o.ap()[:, sp * SPAN:(sp + 1) * SPAN], cT[:]), 16)
            p.barrier([nc.gpsimd])
            for n in range(NST):
                p.inc("dg", nc.gpsimd.dma_start(
                    reps[:, (2 * n) * SPAN:(2 * n + 1) * SPAN],
                    reps_d.ap()[n:n + 1, :].broadcast_to([128, SPAN])), 16)
                p.inc("dg", nc.gpsimd.dma_start(
                    reps[:, (2 * n + 1) * SPAN:(2 * n + 2) * SPAN],
                    reps_d.ap()[NST + n:NST + n + 1, :].broadcast_to([128, SPAN])), 16)
            p.barrier([nc.vector, nc.scalar, nc.tensor])
            import os as _os
            if _os.environ.get("KDBG") and sp == 0:
                reps0_o = p.dout("d_reps0", (128, 2 * NST * SPAN), F32)
                car0_o = p.dout("d_car0", (128, 128), F32)
                zc0_o = p.dout("d_zc0", (128, NC8 * SPAN), BF16)
                wsp0_o = p.dout("d_wsp0", (128, JT * NST), F32)
                ut0_o = p.dout("d_ut0", (128, NC8 * SPAN), BF16)
                xln0_o = p.dout("d_xln0", (128, JT * D), BF16)
                xlnT0_o = p.dout("d_xlnT0", (128, NC8 * SPAN), BF16)
                dtp0_o = p.dout("d_dtp0", (128, NC8 * SPAN), F32)
                p.barrier([nc.sync])
                p.inc("ds", nc.sync.dma_start(reps0_o.ap(), reps[:]), 16)
                p.inc("ds", nc.sync.dma_start(car0_o.ap(), carry[:]), 16)
                p.inc("ds", nc.sync.dma_start(zc0_o.ap(), zc[:]), 16)
                p.inc("ds", nc.sync.dma_start(wsp0_o.ap(), wsp[:]), 16)
                p.inc("ds", nc.sync.dma_start(ut0_o.ap(), uTb[:]), 16)
                p.inc("ds", nc.sync.dma_start(xln0_o.ap(), xln[:]), 16)
                p.inc("ds", nc.sync.dma_start(xlnT0_o.ap(), xlnT[:]), 16)
                p.inc("ds", nc.sync.dma_start(dtp0_o.ap(), dtpreT[:]), 16)
                p.barrier()

            scanmk = {}
            ttmk = {}
            ymmk = {}

            def emit_y_evac(cc):
                yslot = (cc % 2) * 512
                p.inc("t", nc.tensor.matmul(
                    ppy[:, yslot:yslot + 512], ident[:],
                    dpu[:, cc * SPAN:(cc + 1) * SPAN], start=False, stop=True))
                p.wait(nc.vector, "t")
                p.inc("v", nc.vector.tensor_copy(ybf[:], ppy[:, yslot:yslot + 512]))
                p.wait(nc.tensor, "v")
                for j in range(JT):
                    p.inc("t", nc.tensor.transpose(
                        ppt[:, 0:128], ybf[:, j * 128:(j + 1) * 128], ident[:]))
                    p.wait(nc.vector, "t")
                    p.inc("v", nc.vector.scalar_tensor_tensor(
                        s1sp[:, j * D + cc * 128: j * D + (cc + 1) * 128],
                        ppt[:, 0:128], 1.0,
                        xsp[:, j * D + cc * 128: j * D + (cc + 1) * 128],
                        ALU.mult, ALU.add))
                    p.wait(nc.tensor, "v")

            def process_prev(prev):
                q, cc, nn = prev
                gg = cc * NST + nn
                if gg >= 2:
                    p.wait(nc.vector, "t", ymmk[gg - 2])
                ttmk[gg] = p.inc("v", nc.vector.tensor_tensor(
                    wtt[:, q * SPAN:(q + 1) * SPAN],
                    hbuf[:, q * SPAN:(q + 1) * SPAN],
                    reps[:, (2 * nn + 1) * SPAN:(2 * nn + 2) * SPAN], ALU.mult))
                p.inc("v", nc.vector.tensor_copy(
                    carry[:, gg:gg + 1],
                    hbuf[:, q * SPAN + SPAN - 1: q * SPAN + SPAN]))
                p.wait(nc.tensor, "v", ttmk[gg])
                ymmk[gg] = p.inc("t", nc.tensor.matmul(
                    ppy[:, (cc % 2) * 512:(cc % 2) * 512 + 512], ident[:],
                    wtt[:, q * SPAN:(q + 1) * SPAN],
                    start=(nn == 0), stop=False))
                if nn == NST - 1:
                    emit_y_evac(cc)

            prev = None
            for c in range(NC8):
                for n in range(NST):
                    g = c * NST + n
                    pp = g % 2
                    if g >= 2:
                        p.wait(nc.scalar, "v", scanmk[g - 2])
                    p.inc("a", nc.scalar.activation(
                        barA[:, pp * SPAN:(pp + 1) * SPAN],
                        dtpreT[:, c * SPAN:(c + 1) * SPAN], AF.Exp,
                        scale=sAc[:, g:g + 1], bias=sBc[:, g:g + 1]))
                    p.wait(nc.vector, "a")
                    scanmk[g] = p.inc("v", nc.vector.tensor_tensor_scan(
                        hbuf[:, pp * SPAN:(pp + 1) * SPAN],
                        barA[:, pp * SPAN:(pp + 1) * SPAN],
                        reps[:, (2 * n) * SPAN:(2 * n + 1) * SPAN],
                        carry[:, g:g + 1], ALU.mult, ALU.add))
                    if prev is not None:
                        process_prev(prev)
                    prev = (pp, c, n)
                    import os as _os
                    if _os.environ.get("KDBG") and sp == 0 and g == 2:
                        h00_o = p.dout("d_h00", (128, SPAN), F32)
                        w00_o = p.dout("d_w00", (128, SPAN), BF16)
                        p.wait(nc.sync, "v", ttmk[0] + 1)
                        p.inc("ds", nc.sync.dma_start(h00_o.ap(), hbuf[:, 0:SPAN]), 16)
                        p.inc("ds", nc.sync.dma_start(w00_o.ap(), wtt[:, 0:SPAN]), 16)
            p.inc("v", nc.vector.engine_nop())
            process_prev(prev)

            p.barrier([nc.sync])
            p.inc("ds", nc.sync.dma_start(
                s1p.ap().rearrange("(i p) m -> p i m", p=128)[:, JT * sp:JT * (sp + 1), :],
                s1sp.ap().rearrange("p (i m) -> p i m", i=JT)), 16)
            p.inc("ds", nc.sync.dma_start(cdbg_o.ap()[sp], carry[:]), 16)
            p.barrier()

        p.inc("ds", nc.sync.dma_start(hend_o.ap(), carry[:]), 16)
        import os
        if os.environ.get("KDBG"):
            for nm, t_ in [("d_uTb", uTb), ("d_dtpreT", dtpreT), ("d_wsp", wsp),
                           ("d_inpT", inpT), ("d_reps", reps), ("d_barA", barA),
                           ("d_hbuf", hbuf), ("d_sAc", sAc), ("d_sBc", sBc),
                           ("d_xlnT", xlnT), ("d_dtlT", dtlT), ("d_xsp", xsp), ("d_xln", xln)]:
                o_ = p.dout(nm, tuple(t_.ap().shape), t_.ap().dtype)
                p.inc("ds", nc.sync.dma_start(o_.ap(), t_[:]), 16)
        p.barrier()
    return p



def build_L2():
    p = Prog()
    nc = p.nc
    SPAN = 512
    NSPAN = T // SPAN
    JT = SPAN // 128

    s1p_in = p.din("s1p", (T, D), F32)
    hrecv_in = p.din("hrecv", (128, 128), F32)
    dtlt_in = p.din("dtlt", (RK, T), BF16)
    ctr_in = p.din("ctr", (NST, T), BF16)
    w_dt = p.din("w_dt", (RK, D), F32)
    b_dt = p.din("b_dt", (D,), F32)
    A_log = p.din("A_log", (D, NST), F32)
    wr_in = p.din("wrin", (D, 8), F32)
    br_in = p.din("brin", (8,), F32)
    ident_in = p.din("identin", (128, 128), BF16)
    ut_in = p.din("utincin", (128, 128), BF16)
    iota8_in = p.din("iota8in", (128, 8), F32)

    s1_o = p.dout("s1", (T, D), F32)
    idx_o = p.dout("idxf_o", (NT, 128), F32)
    gate_o = p.dout("gate_o", (NT, 128), F32)
    selpos_o = p.dout("selpos_o", (NT, 128), F32)
    cnt_o = p.dout("cnt_o", (1, 8), F32)

    with p.stack:
        for s in ("t", "v", "a", "ds", "dg"):
            p.sem(s)

        wdt = p.sb("wdt", (RK, D), BF16)
        bdtr = p.sb("bdtr", (1, D), BF16)
        alc = p.sb("alc", (128, 128), F32)
        sAf = p.sb("sAf", (128, 128), F32)
        hrc = p.sb("hrc", (128, 128), F32)
        ident = p.sb("ident", (128, 128), BF16)
        utinc = p.sb("utinc", (128, 128), BF16)
        iota8 = p.sb("iota8", (128, 8), F32)
        wr = p.sb("wr", (128, NC8 * 8), BF16)
        brr = p.sb("brr", (1, 8), BF16)
        ones_bf = p.sb("ones_bf", (1, SPAN), BF16)
        onescol = p.sb("onescol", (128, 1), BF16)
        cst = p.sb("cst", (128, 2), F32)
        nc.vector.memset(cst[:, 0:1], 1e-5)
        p.inc("v", nc.vector.memset(cst[:, 1:2], -1.0))
        p.cst_eps = cst[:, 0:1]
        p.cst_neg1 = cst[:, 1:2]

        p.inc("dg", nc.gpsimd.dma_start(wdt[:], w_dt.ap()), 16)
        p.inc("dg", nc.gpsimd.dma_start(bdtr[:], b_dt.ap().unsqueeze(0)), 16)
        p.inc("dg", nc.gpsimd.dma_start(
            wr.ap().rearrange("p (k m) -> p k m", k=NC8),
            wr_in.ap().rearrange("(k p) m -> p k m", p=128)), 16)
        p.inc("dg", nc.gpsimd.dma_start(brr[:], br_in.ap().unsqueeze(0)), 16)
        p.inc("ds", nc.sync.dma_start(
            alc.ap().rearrange("p (c n) -> p c n", c=NC8),
            A_log.ap().rearrange("(c p) n -> p c n", p=128)), 16)
        p.inc("ds", nc.sync.dma_start(hrc[:], hrecv_in.ap()), 16)
        p.inc("ds", nc.sync.dma_start(ident[:], ident_in.ap()), 16)
        p.inc("ds", nc.sync.dma_start(utinc[:], ut_in.ap()), 16)
        p.inc("ds", nc.sync.dma_start(iota8[:], iota8_in.ap()), 16)
        p.inc("v", nc.vector.memset(ones_bf[:], 1.0))
        p.inc("v", nc.vector.memset(onescol[:], 1.0))
        p.barrier()
        p.inc("a", nc.scalar.activation(sAf[:], alc[:], AF.Exp))
        p.wait(nc.vector, "a")
        p.inc("v", nc.vector.tensor_scalar_mul(sAf[:], sAf[:], -1.0))
        p.barrier()

        xsp = p.sb("xsp", (128, JT * D), F32)
        lnsq = p.sb("lnsq", (128, D), F32)
        dtlT = p.sb("dtlT", (RK, SPAN), BF16)
        sp1 = p.sb("sp1", (128, SPAN), F32)
        sp2 = p.sb("sp2", (128, SPAN), F32)
        dtTc = p.sb("dtTc", (128, 2 * SPAN), F32)
        onesf = p.sb("onesf", (128, SPAN), F32)
        csT = p.sb("csT", (128, 2 * SPAN), F32)
        cscar = p.sb("cscar", (128, NC8), F32)
        phi = p.sb("phi", (128, 4 * SPAN), F32)
        w2b = p.sb("w2b", (128, 4 * SPAN), BF16)
        crep = p.sb("crep", (128, NST * SPAN), BF16)
        ybf = p.sb("ybf", (128, SPAN), BF16)
        s1sp = p.sb("s1sp", (128, JT * D), F32)
        xln = p.sb("xln", (128, JT * D), BF16)
        lg = p.sb("lg", (128, 8), F32)
        mcol = p.sb("mcol", (128, 8), F32)
        idx8 = p.sb("idx8", (128, 8), mybir.dt.uint32)
        idxf = p.sb("idxf", (128, 1), F32)
        oh = p.sb("oh", (128, 8), F32)
        ohb = p.sb("ohb", (128, 8), BF16)
        posg = p.sb("posg", (128, 8), F32)
        selc = p.sb("selc", (128, 8), F32)
        runr = p.sb("runr", (1, 8), F32)
        runb = p.sb("runb", (1, 8), BF16)
        totr = p.sb("totr", (1, 8), F32)
        exps = p.sb("exps", (128, 8), F32)

        ppb = p.ps("ppb", (128, 1024), F32)
        ppy = p.ps("ppy", (128, 1024), F32)
        ppr = p.ps("ppr", (128, 512), F32)
        ppbt = p.ps("ppbt", (128, 1024), BF16)

        p.inc("v", nc.vector.memset(onesf[:], 1.0))
        p.inc("v", nc.vector.memset(cscar[:], 0.0))
        p.inc("v", nc.vector.memset(runr[:], 0.0))
        p.inc("v", nc.vector.memset(runb[:], 0.0))
        p.barrier()

        for sp in range(NSPAN):
            p.inc("ds", nc.sync.dma_start(
                xsp.ap().rearrange("p (i m) -> p i m", i=JT),
                s1p_in.ap().rearrange("(i p) m -> p i m", p=128)[:, JT * sp:JT * (sp + 1), :]), 16)
            p.inc("ds", nc.sync.dma_start(dtlT[:], dtlt_in.ap()[:, sp * SPAN:(sp + 1) * SPAN]), 16)
            for n in range(NST):
                p.inc("dg", nc.gpsimd.dma_start(
                    crep[:, n * SPAN:(n + 1) * SPAN],
                    ctr_in.ap()[n:n + 1, sp * SPAN:(sp + 1) * SPAN].broadcast_to([128, SPAN])), 16)
            p.barrier()

            scanmk = {}
            ttmk = {}
            ymmk = {}
            for c in range(NC8):
                slot = (c % 2) * 512
                yslot = (c % 2) * 512
                if c >= 2:
                    p.wait(nc.tensor, "a")
                p.inc("t", nc.tensor.matmul(ppb[:, slot:slot + 512],
                                            wdt[:, c * 128:(c + 1) * 128], dtlT[:],
                                            start=True, stop=False))
                p.inc("t", nc.tensor.matmul(ppb[:, slot:slot + 512],
                                            bdtr[:, c * 128:(c + 1) * 128], ones_bf[:],
                                            start=False, stop=True))
                p.wait(nc.scalar, "t")
                p.wait(nc.scalar, "v")
                nc.scalar.activation(sp1[:], ppb[:, slot:slot + 512], AF.Abs)
                nc.scalar.activation(sp1[:], sp1[:], AF.Exp, scale=p.cst_neg1)
                nc.scalar.activation(sp1[:], sp1[:], AF.Ln, bias=1.0)
                p.inc("a", nc.scalar.activation(sp2[:], ppb[:, slot:slot + 512], AF.Relu))
                p.wait(nc.vector, "a")
                p.inc("v", nc.vector.tensor_tensor(
                    dtTc[:, slot:slot + 512], sp1[:], sp2[:], ALU.add))
                p.wait(nc.vector, "v")
                p.inc("v", nc.vector.tensor_scalar(
                    dtTc[:, slot:slot + 512], dtTc[:, slot:slot + 512], 0.099, 0.001,
                    ALU.mult, ALU.add))
                p.wait(nc.vector, "v")
                csmk = p.inc("v", nc.vector.tensor_tensor_scan(
                    csT[:, slot:slot + 512], onesf[:], dtTc[:, slot:slot + 512],
                    cscar[:, c:c + 1], ALU.mult, ALU.add))
                p.wait(nc.vector, "v", csmk)
                p.inc("v", nc.vector.tensor_copy(
                    cscar[:, c:c + 1], csT[:, slot + 512 - 1:slot + 512]))
                for n in range(NST):
                    g = c * NST + n
                    pp = g % 4
                    if g >= 4:
                        p.wait(nc.scalar, "v", scanmk.get(g - 4, 0))
                    p.wait(nc.scalar, "v", csmk)
                    p.inc("a", nc.scalar.activation(
                        phi[:, pp * SPAN:(pp + 1) * SPAN],
                        csT[:, slot:slot + 512], AF.Exp,
                        scale=sAf[:, g:g + 1]))
                    p.wait(nc.vector, "a")
                    if g >= 4:
                        p.wait(nc.vector, "t", ymmk.get(g - 4, 0))
                    scanmk[g] = p.inc("v", nc.vector.scalar_tensor_tensor(
                        w2b[:, pp * SPAN:(pp + 1) * SPAN],
                        phi[:, pp * SPAN:(pp + 1) * SPAN], hrc[:, g:g + 1],
                        crep[:, n * SPAN:(n + 1) * SPAN], ALU.mult, ALU.mult))
                    p.wait(nc.tensor, "v", scanmk[g])
                    ymmk[g] = p.inc("t", nc.tensor.matmul(
                        ppy[:, yslot:yslot + 512], ident[:],
                        w2b[:, pp * SPAN:(pp + 1) * SPAN],
                        start=(n == 0), stop=(n == NST - 1)))
                p.wait(nc.vector, "t")
                p.inc("v", nc.vector.tensor_copy(ybf[:], ppy[:, yslot:yslot + 512]))
                p.wait(nc.tensor, "v")
                for j in range(JT):
                    p.inc("t", nc.tensor.transpose(
                        ppbt[:, 0:128], ybf[:, j * 128:(j + 1) * 128], ident[:]))
                    p.wait(nc.vector, "t")
                    p.inc("v", nc.vector.scalar_tensor_tensor(
                        s1sp[:, j * D + c * 128: j * D + (c + 1) * 128],
                        ppbt[:, 0:128], 1.0,
                        xsp[:, j * D + c * 128: j * D + (c + 1) * 128],
                        ALU.mult, ALU.add))
                    p.wait(nc.tensor, "v")
            p.wait(nc.sync, "v")
            p.inc("ds", nc.sync.dma_start(
                s1_o.ap().rearrange("(i p) m -> p i m", p=128)[:, JT * sp:JT * (sp + 1), :],
                s1sp.ap().rearrange("p (i m) -> p i m", i=JT)), 16)

            for j in range(JT):
                ti = sp * JT + j
                emit_ln(p, f"r{sp}_{j}", s1sp[:, j * D:(j + 1) * D],
                        xln[:, j * D:(j + 1) * D], lnsq)
                p.wait(nc.tensor, "v")
                for c in range(NC8):
                    p.inc("t", nc.tensor.transpose(
                        ppbt[:, 0:128], xln[:, j * D + c * 128: j * D + (c + 1) * 128],
                        ident[:]))
                    p.wait(nc.vector, "t")
                    p.inc("v", nc.vector.tensor_copy(
                        w2b[:, 0:128], ppbt[:, 0:128]))
                    p.wait(nc.tensor, "v")
                    p.inc("t", nc.tensor.matmul(
                        ppr[:, 0:8], w2b[:, 0:128], wr[:, c * 8:(c + 1) * 8],
                        start=(c == 0), stop=False))
                p.inc("t", nc.tensor.matmul(
                    ppr[:, 0:8], ones_bf[0:1, 0:128], brr[:],
                    start=False, stop=True))
                p.wait(nc.vector, "t")
                p.inc("v", nc.vector.tensor_copy(lg[:], ppr[:, 0:8]))
                p.wait(nc.vector, "v")
                p.inc("v", nc.vector.reduce_max(mcol[:, 0:1], lg[:], AX.X))
                p.wait(nc.vector, "v")
                p.inc("v", nc.vector.tensor_scalar_mul(mcol[:, 1:2], mcol[:, 0:1], -1.0))
                p.wait(nc.scalar, "v")
                p.inc("a", nc.scalar.activation(exps[:], lg[:], AF.Exp,
                                                bias=mcol[:, 1:2], accum_out=mcol[:, 2:3]))
                p.wait(nc.vector, "a")
                p.inc("v", nc.vector.reciprocal(mcol[:, 3:4], mcol[:, 2:3]))
                p.inc("v", nc.vector.max_index(idx8[:], mcol[:, 0:1].broadcast_to([128, 8]), lg[:]))
                p.wait(nc.vector, "v")
                p.inc("v", nc.vector.tensor_copy(idxf[:], idx8[:, 0:1]))
                p.wait(nc.vector, "v")
                p.inc("v", nc.vector.tensor_tensor(
                    oh[:], idxf.ap().broadcast_to([128, 8]), iota8[:], ALU.is_equal))
                p.wait(nc.vector, "v")
                p.inc("v", nc.vector.tensor_copy(ohb[:], oh[:]))
                p.wait(nc.tensor, "v")
                p.inc("t", nc.tensor.matmul(ppr[:, 8:16], utinc[:], ohb[:],
                                            start=True, stop=False))
                p.inc("t", nc.tensor.matmul(ppr[:, 8:16], ones_bf[0:1, 0:128], runb[:],
                                            start=False, stop=True))
                p.inc("t", nc.tensor.matmul(ppr[0:1, 16:24], onescol[:], ohb[:],
                                            start=True, stop=True))
                p.wait(nc.vector, "t")
                p.inc("v", nc.vector.tensor_tensor(posg[:], ppr[:, 8:16], oh[:], ALU.mult))
                p.wait(nc.vector, "v")
                p.inc("v", nc.vector.tensor_reduce(selc[:, 0:1], posg[:], AX.X, ALU.add))
                p.inc("v", nc.vector.tensor_tensor(runr[:], runr[:], ppr[0:1, 16:24], ALU.add))
                p.wait(nc.vector, "v")
                p.inc("v", nc.vector.tensor_copy(runb[:], runr[:]))
                p.wait(nc.vector, "v")
                p.inc("v", nc.vector.tensor_copy(mcol[:, 4:5], mcol[:, 3:4]))
                p.wait(nc.sync, "v")
                p.inc("ds", nc.sync.dma_start(idx_o.ap()[ti:ti + 1, :], idxf[:]), 16)
                p.inc("ds", nc.sync.dma_start(gate_o.ap()[ti:ti + 1, :], mcol[:, 3:4]), 16)
                p.inc("ds", nc.sync.dma_start(selpos_o.ap()[ti:ti + 1, :], selc[:, 0:1]), 16)
            p.barrier()

        p.wait(nc.sync, "v")
        p.inc("ds", nc.sync.dma_start(cnt_o.ap(), runr[:]), 16)
        import os as _os
        if _os.environ.get("KDBG"):
            for nm, t_ in [("d2_sAf", sAf), ("d2_cscar", cscar), ("d2_csT", csT),
                           ("d2_phi", phi), ("d2_hrc", hrc), ("d2_dtTc", dtTc)]:
                o_ = p.dout(nm, tuple(t_.ap().shape), t_.ap().dtype)
                p.inc("ds", nc.sync.dma_start(o_.ap(), t_[:]), 16)
        p.barrier()
    return p



def build_L3():
    p = Prog()
    nc = p.nc
    SPAN = 512
    NSPAN = T // SPAN
    JT = SPAN // 128
    E = 8

    s1_in = p.din("s1", (T, D), F32)
    idx_in = p.din("idxf_i", (NT, 128), F32)
    gate_in = p.din("gate_i", (NT, 128), F32)
    selpos_in = p.din("selpos_i", (NT, 128), F32)
    allcnt_in = p.din("allcnt", (8, 8), F32)
    cmask_in = p.din("cmask", (8, 1), F32)
    w1_in = p.din("w1", (E, D, D), F32)
    b1_in = p.din("b1", (E, D), F32)
    w2_in = p.din("w2", (E, D, D), F32)
    b2_in = p.din("b2", (E, D), F32)
    win_in = p.din("winh", (D, 3072), F32)
    bin_in = p.din("binh", (3072,), F32)
    wconv_in = p.din("wconv", (3072, 1, 3), F32)
    bconv_in = p.din("bconv", (3072,), F32)
    ident_in = p.din("identin", (128, 128), BF16)
    iota8_in = p.din("iota8in", (128, 8), F32)

    s2_o = p.dout("s2", (T, D), F32)
    v_o = p.dout("vfull", (T, D), BF16)
    s0_o = p.dout("s0ct", (D, T), BF16)
    uh_o = p.dout("uht", (128, 64), BF16)
    c8_d = p.dscratch("c8d", (8, D), F32)
    xlnt_d = p.dscratch("xlntd", (D, T), BF16)
    sacc_d = p.dscratch("saccd", (D, T), F32)

    with p.stack:
        for s in ("t", "v", "a", "ds", "dg"):
            p.sem(s)

        ident = p.sb("ident", (128, 128), BF16)
        iota8 = p.sb("iota8", (128, 8), F32)
        cst = p.sb("cst", (128, 2), F32)
        nc.vector.memset(cst[:, 0:1], 1e-5)
        p.inc("v", nc.vector.memset(cst[:, 1:2], -1.0))
        p.cst_eps = cst[:, 0:1]
        p.cst_neg1 = cst[:, 1:2]
        ones_bf = p.sb("ones_bf", (1, SPAN), BF16)
        onescol = p.sb("onescol", (8, 1), BF16)
        onef = p.sb("onef", (1, 1), BF16)
        p.inc("v", nc.vector.memset(ones_bf[:], 1.0))
        p.inc("v", nc.vector.memset(onescol[:], 1.0))
        p.inc("v", nc.vector.memset(onef[:], 1.0))
        p.inc("ds", nc.sync.dma_start(ident[:], ident_in.ap()), 16)
        p.inc("ds", nc.sync.dma_start(iota8[:], iota8_in.ap()), 16)

        acnt = p.sb("acnt", (8, 8), F32)
        cmsk = p.sb("cmsk", (8, 1), F32)
        pref = p.sb("pref", (1, 8), F32)
        prep = p.sb("prep", (128, 8), F32)
        onesc128 = p.sb("onesc128", (1, 128), F32)
        p.inc("v", nc.vector.memset(onesc128[:], 1.0))
        p.inc("ds", nc.sync.dma_start(acnt[:], allcnt_in.ap()), 16)
        p.inc("ds", nc.sync.dma_start(cmsk[:], cmask_in.ap()), 16)
        ppr = p.ps("ppr", (128, 512), F32)
        p.barrier()
        p.inc("t", nc.tensor.matmul(ppr[0:1, 0:8], cmsk[:], acnt[:], start=True, stop=True))
        p.wait(nc.vector, "t")
        p.inc("v", nc.vector.tensor_copy(pref[:], ppr[0:1, 0:8]))
        p.wait(nc.tensor, "v")
        p.inc("t", nc.tensor.matmul(ppr[:, 0:8], onesc128[:], pref[:], start=True, stop=True))
        p.wait(nc.vector, "t")
        p.inc("v", nc.vector.tensor_copy(prep[:], ppr[:, 0:8]))
        p.barrier()

        xsp = p.sb("xsp", (128, JT * D), F32)
        lnsq = p.sb("lnsq", (128, D), F32)
        xlsp = p.sb("xlsp", (128, NC8 * SPAN), BF16)
        xln = p.sb("xln", (128, JT * D), BF16)
        rcol = p.sb("rcol", (128, 8), F32)
        oh = p.sb("oh", (128, 8), F32)
        ohg = p.sb("ohg", (128, 8), BF16)
        gm8 = p.sb("gm8", (8, T), BF16)
        ones8 = p.sb("ones8", (8, 128), BF16)
        p.inc("v", nc.vector.memset(ones8[:], 1.0))
        ppt = p.ps("ppt", (128, 2048), BF16)

        for sp in range(NSPAN):
            p.wait(nc.sync, "v")
            p.wait(nc.sync, "a")
            p.inc("ds", nc.sync.dma_start(
                xsp.ap().rearrange("p (i m) -> p i m", i=JT),
                s1_in.ap().rearrange("(i p) m -> p i m", p=128)[:, JT * sp:JT * (sp + 1), :]), 16)
            p.barrier([nc.vector, nc.scalar])
            for j in range(JT):
                ti = sp * JT + j
                emit_ln(p, f"f{sp}_{j}", xsp[:, j * D:(j + 1) * D],
                        xln[:, j * D:(j + 1) * D], lnsq)
                p.wait(nc.sync, "v")
                p.inc("ds", nc.sync.dma_start(rcol[:, 0:1], idx_in.ap()[ti:ti + 1, :]), 16)
                p.inc("ds", nc.sync.dma_start(rcol[:, 1:2], gate_in.ap()[ti:ti + 1, :]), 16)
                p.inc("ds", nc.sync.dma_start(rcol[:, 2:3], selpos_in.ap()[ti:ti + 1, :]), 16)
                p.barrier([nc.vector])
                p.inc("v", nc.vector.tensor_tensor(
                    oh[:], rcol[:, 0:1].broadcast_to([128, 8]), iota8[:], ALU.is_equal))
                p.wait(nc.vector, "v")
                p.inc("v", nc.vector.scalar_tensor_tensor(
                    lnsq[:, 0:8], oh[:], 1.0, prep[:], ALU.mult, ALU.mult))
                p.wait(nc.vector, "v")
                p.inc("v", nc.vector.tensor_reduce(rcol[:, 3:4], lnsq[:, 0:8], AX.X, ALU.add))
                p.wait(nc.vector, "v")
                p.inc("v", nc.vector.tensor_tensor(rcol[:, 4:5], rcol[:, 2:3], rcol[:, 3:4], ALU.add))
                p.wait(nc.vector, "v")
                p.inc("v", nc.vector.tensor_single_scalar(rcol[:, 5:6], rcol[:, 4:5], 4096.0, ALU.is_lt))
                p.wait(nc.vector, "v")
                p.inc("v", nc.vector.tensor_tensor(rcol[:, 6:7], rcol[:, 1:2], rcol[:, 5:6], ALU.mult))
                p.wait(nc.vector, "v")
                p.inc("v", nc.vector.tensor_scalar_mul(ohg[:], oh[:], rcol[:, 6:7]))
                p.wait(nc.tensor, "v")
                p.inc("t", nc.tensor.transpose(ppt[0:8, 0:128], ohg[:], ident[:]))
                p.wait(nc.vector, "t")
                p.inc("v", nc.vector.tensor_copy(gm8[:, ti * 128:(ti + 1) * 128], ppt[0:8, 0:128]))
                p.wait(nc.tensor, "v")
                for c in range(NC8):
                    slot = (c % 2) * 1024
                    p.inc("t", nc.tensor.transpose(
                        ppt[:, slot:slot + 128],
                        xln[:, j * D + c * 128: j * D + (c + 1) * 128], ident[:]))
                    p.wait(nc.vector, "t")
                    p.inc("v", nc.vector.tensor_copy(
                        xlsp[:, c * SPAN + j * 128: c * SPAN + (j + 1) * 128],
                        ppt[:, slot:slot + 128]))
                    if c < NC8 - 1:
                        p.wait(nc.tensor, "v", p.val("v") - 1)
            p.wait(nc.sync, "v")
            p.inc("ds", nc.sync.dma_start(
                xlnt_d.ap().rearrange("(c p) t -> p c t", p=128)[:, :, sp * SPAN:(sp + 1) * SPAN],
                xlsp.ap().rearrange("p (c t) -> p c t", c=NC8)), 16)
        p.barrier()

        w1e = p.sb("w1e", (128, 2 * NC8 * D), BF16)
        w2e = p.sb("w2e", (128, 2 * NC8 * D), BF16)
        b1r = p.sb("b1r", (1, 2 * D), BF16)
        b2r = p.sb("b2r", (1, 2 * D), BF16)
        rb1 = p.sb("rb1", (128, 2 * NC8), F32)
        rb1b = p.sb("rb1b", (128, 2 * NC8), BF16)
        seltab = p.sb("seltab", (8, 8 * 128), BF16)
        p.wait(nc.vector, "ds")
        for e_ in range(8):
            p.inc("v", nc.vector.tensor_copy(
                seltab[:, e_ * 128:(e_ + 1) * 128],
                ident[0:8, e_:e_ + 1].broadcast_to([8, 128])))
            p.wait(nc.vector, "v")
        xe = p.sb("xe", (128, NC8 * SPAN), BF16)
        h1b = p.sb("h1b", (128, NC8 * SPAN), BF16)
        saccsp = p.sb("saccsp", (128, 2 * SPAN), F32)
        aidx, amk = 0, {}
        ppb = p.ps("ppb", (128, 2048), F32)
        ppg = p.ps("ppg", (128, 512), F32)

        def load_expert(e, sl):
            WO = sl * NC8 * D
            p.inc("dg", nc.gpsimd.dma_start(
                w1e.ap()[:, WO:WO + NC8 * D].rearrange("p (k m) -> p k m", k=NC8),
                w1_in.ap()[e].rearrange("(k p) m -> p k m", p=128)), 16)
            p.inc("dg", nc.gpsimd.dma_start(
                w2e.ap()[:, WO:WO + NC8 * D].rearrange("p (k m) -> p k m", k=NC8),
                w2_in.ap()[e].rearrange("(k p) m -> p k m", p=128)), 16)
            p.inc("dg", nc.gpsimd.dma_start(b1r[0:1, sl * D:(sl + 1) * D],
                                            b1_in.ap()[e].unsqueeze(0)), 16)
            p.inc("dg", nc.gpsimd.dma_start(b2r[0:1, sl * D:(sl + 1) * D],
                                            b2_in.ap()[e].unsqueeze(0)), 16)
            for c in range(NC8):
                p.inc("ds", nc.sync.dma_start(
                    rb1[:, sl * NC8 + c: sl * NC8 + c + 1],
                    b1_in.ap()[e, c * 128:(c + 1) * 128].unsqueeze(-1)), 16)
            return (p.val("dg"), p.val("ds"))

        wmarks = {0: load_expert(0, 0)}
        for e in range(E):
            sl = e % 2
            WO = sl * NC8 * D
            gmk, smk = wmarks[e]
            for eng in (nc.tensor, nc.vector):
                p.wait(eng, "dg", gmk)
                p.wait(eng, "ds", smk)
            if e + 1 < E:
                wmarks[e + 1] = load_expert(e + 1, 1 - sl)
            p.inc("v", nc.vector.tensor_scalar_max(
                rb1[:, sl * NC8:(sl + 1) * NC8], rb1[:, sl * NC8:(sl + 1) * NC8], 0.0))
            p.wait(nc.vector, "v")
            p.inc("v", nc.vector.tensor_copy(rb1b[:, sl * NC8:(sl + 1) * NC8],
                                             rb1[:, sl * NC8:(sl + 1) * NC8]))
            p.barrier([nc.tensor])
            for half in range(2):
                for k in range(NC8):
                    p.inc("t", nc.tensor.matmul(
                        ppb[0:1, half * 512:half * 512 + 512],
                        rb1b[:, sl * NC8 + k: sl * NC8 + k + 1],
                        w2e[:, WO + k * D + half * 512: WO + k * D + half * 512 + 512],
                        start=(k == 0), stop=False))
                p.inc("t", nc.tensor.matmul(
                    ppb[0:1, half * 512:half * 512 + 512],
                    onef[:], b2r[0:1, sl * D + half * 512: sl * D + half * 512 + 512],
                    start=False, stop=True))
            p.wait(nc.vector, "t")
            p.inc("v", nc.vector.tensor_copy(lnsq[0:1, 0:D], ppb[0:1, 0:1024]))
            p.wait(nc.sync, "v")
            p.inc("ds", nc.sync.dma_start(c8_d.ap()[e:e + 1, :], lnsq[0:1, 0:D]), 16)

            for sp in range(NSPAN):
                p.wait(nc.tensor, "v")
                p.barrier([nc.sync])
                p.inc("ds", nc.sync.dma_start(
                    xlsp.ap().rearrange("p (c t) -> p c t", c=NC8),
                    xlnt_d.ap().rearrange("(c p) t -> p c t", p=128)[:, :, sp * SPAN:(sp + 1) * SPAN]), 16)
                p.inc("t", nc.tensor.matmul(
                    ppg[:], seltab[:, e * 128:(e + 1) * 128],
                    gm8[:, sp * SPAN:(sp + 1) * SPAN],
                    start=True, stop=True))
                p.wait(nc.vector, "t")
                p.barrier([nc.vector])
                for c in range(NC8):
                    p.inc("v", nc.vector.tensor_tensor(
                        xe[:, c * SPAN:(c + 1) * SPAN],
                        xlsp[:, c * SPAN:(c + 1) * SPAN],
                        ppg[:], ALU.mult))
                p.wait(nc.tensor, "v")
                evk = {}
                for mf in range(NC8):
                    slot = (mf % 4) * 512
                    if mf == 4:
                        p.wait(nc.tensor, "v", evk[3])
                    for k in range(NC8):
                        p.inc("t", nc.tensor.matmul(
                            ppb[:, slot:slot + 512],
                            w1e[:, WO + k * D + mf * 128: WO + k * D + (mf + 1) * 128],
                            xe[:, k * SPAN:(k + 1) * SPAN],
                            start=(k == 0), stop=False))
                    p.inc("t", nc.tensor.matmul(
                        ppb[:, slot:slot + 512],
                        b1r[0:1, sl * D + mf * 128: sl * D + (mf + 1) * 128], ones_bf[:],
                        start=False, stop=True))
                    p.wait(nc.vector, "t")
                    evk[mf] = p.inc("v", nc.vector.scalar_tensor_tensor(
                        h1b[:, mf * SPAN:(mf + 1) * SPAN],
                        ppb[:, slot:slot + 512], 0.0,
                        rb1[:, sl * NC8 + mf: sl * NC8 + mf + 1].broadcast_to([128, SPAN]),
                        ALU.max, ALU.subtract))
                p.wait(nc.tensor, "v")
                cpk = {}
                for md in range(NC8):
                    slot = (md % 4) * 512
                    if md == 4:
                        p.wait(nc.tensor, "v", cpk[3])
                    for k in range(NC8):
                        p.inc("t", nc.tensor.matmul(
                            ppb[:, slot:slot + 512],
                            w2e[:, WO + k * D + md * 128: WO + k * D + (md + 1) * 128],
                            h1b[:, k * SPAN:(k + 1) * SPAN],
                            start=(k == 0), stop=(k == NC8 - 1)))
                    p.wait(nc.vector, "t")
                    sl4 = (aidx % 2) * 512
                    if aidx >= 2:
                        p.wait(nc.vector, "dg", amk[aidx - 2])
                    cpk[md] = p.inc("v", nc.vector.tensor_copy(
                        saccsp[:, sl4:sl4 + 512], ppb[:, slot:slot + 512]))
                    p.wait(nc.gpsimd, "v")
                    dstap = sacc_d.ap().rearrange("(c p) t -> p c t", p=128)[:, md, sp * SPAN:(sp + 1) * SPAN]
                    if e == 0:
                        amk[aidx] = p.inc("dg", nc.gpsimd.dma_start(dstap, saccsp[:, sl4:sl4 + 512]), 16)
                    else:
                        amk[aidx] = p.inc("dg", nc.gpsimd.dma_start(dstap, saccsp[:, sl4:sl4 + 512], accum_op=ALU.add), 16)
                    aidx += 1
            p.barrier()

        c8sb = p.sb("c8sb", (8, D), BF16)
        s2sp = p.sb("s2sp", (128, JT * D), F32)
        otmp = p.sb("otmp", (128, SPAN), F32)
        ybf = p.sb("ybf", (128, SPAN), BF16)
        p.inc("dg", nc.gpsimd.dma_start(c8sb[:], c8_d.ap()), 16)
        p.inc("dg", nc.gpsimd.dma_start(
            w1e.ap()[:, 0:NC8 * D].rearrange("p (k m) -> p k m", k=NC8),
            win_in.ap()[:, 0:D].rearrange("(k p) m -> p k m", p=128)), 16)
        p.inc("dg", nc.gpsimd.dma_start(
            w2e.ap()[:, 0:NC8 * D].rearrange("p (k m) -> p k m", k=NC8),
            win_in.ap()[:, 2 * D:3 * D].rearrange("(k p) m -> p k m", p=128)), 16)
        u0p = p.sb("u0p", (128, NC8 * (SPAN + 2)), BF16)
        u2p = p.sb("u2p", (128, NC8 * (SPAN + 2)), BF16)
        cu0 = p.sb("cu0", (128, NC8 * SPAN), BF16)
        cu2 = p.sb("cu2", (128, NC8 * SPAN), BF16)
        vtm = p.sb("vtm", (128, JT * D), BF16)
        wcv = p.sb("wcv", (128, 8 * NC8), F32)
        with p.nc.allow_non_contiguous_dma(reason="tiny strided w_conv columns"):
            for c in range(NC8):
                for j in range(3):
                    p.inc("ds", nc.sync.dma_start(
                        wcv[:, c * 8 + j: c * 8 + j + 1],
                        wconv_in.ap()[c * 128:(c + 1) * 128, 0, j:j + 1]), 16)
                    p.inc("ds", nc.sync.dma_start(
                        wcv[:, c * 8 + 4 + j: c * 8 + 5 + j],
                        wconv_in.ap()[2 * D + c * 128: 2 * D + (c + 1) * 128, 0, j:j + 1]), 16)
                p.inc("ds", nc.sync.dma_start(
                    wcv[:, c * 8 + 3: c * 8 + 4],
                    bconv_in.ap()[c * 128:(c + 1) * 128].unsqueeze(-1)), 16)
                p.inc("ds", nc.sync.dma_start(
                    wcv[:, c * 8 + 7: c * 8 + 8],
                    bconv_in.ap()[2 * D + c * 128: 2 * D + (c + 1) * 128].unsqueeze(-1)), 16)
        bic = p.sb("bic", (128, 2 * NC8), F32)
        for c in range(NC8):
            p.inc("ds", nc.sync.dma_start(
                bic[:, c:c + 1], bin_in.ap()[c * 128:(c + 1) * 128].unsqueeze(-1)), 16)
            p.inc("ds", nc.sync.dma_start(
                bic[:, NC8 + c: NC8 + c + 1],
                bin_in.ap()[2 * D + c * 128: 2 * D + (c + 1) * 128].unsqueeze(-1)), 16)
        p.barrier()
        for c in range(NC8):
            for s_ in range(2):
                b0 = c * 8 + 4 * s_
                p.inc("v", nc.vector.tensor_tensor(
                    lnsq[:, 0:1], wcv[:, b0:b0 + 1], wcv[:, b0 + 1:b0 + 2], ALU.add))
                p.wait(nc.vector, "v")
                p.inc("v", nc.vector.tensor_tensor(
                    lnsq[:, 0:1], lnsq[:, 0:1], wcv[:, b0 + 2:b0 + 3], ALU.add))
                p.wait(nc.vector, "v")
                p.inc("v", nc.vector.tensor_tensor(
                    lnsq[:, 0:1], lnsq[:, 0:1],
                    bic[:, s_ * NC8 + c: s_ * NC8 + c + 1], ALU.mult))
                p.wait(nc.vector, "v")
                p.inc("v", nc.vector.tensor_tensor(
                    wcv[:, b0 + 3:b0 + 4], wcv[:, b0 + 3:b0 + 4], lnsq[:, 0:1], ALU.add))
                p.wait(nc.vector, "v")
        p.inc("v", nc.vector.memset(u0p[:], 0.0))
        p.inc("v", nc.vector.memset(u2p[:], 0.0))
        p.barrier()
        for sp in range(NSPAN):
            p.inc("ds", nc.sync.dma_start(
                xsp.ap().rearrange("p (i m) -> p i m", i=JT),
                s1_in.ap().rearrange("(i p) m -> p i m", p=128)[:, JT * sp:JT * (sp + 1), :]), 16)
            p.wait(nc.tensor, "v")
            p.inc("t", nc.tensor.matmul(
                ppg[:], ones8[:],
                gm8[:, sp * SPAN:(sp + 1) * SPAN], start=True, stop=True))
            p.barrier([nc.vector])
            for md in range(NC8):
                p.inc("t", nc.tensor.matmul(
                    ppb[:, 0:512], c8sb[:, md * 128:(md + 1) * 128],
                    gm8[:, sp * SPAN:(sp + 1) * SPAN], start=True, stop=True))
                p.wait(nc.sync, "v")
                p.inc("ds", nc.sync.dma_start(
                    saccsp[:, 0:512],
                    sacc_d.ap().rearrange("(c p) t -> p c t", p=128)[:, md, sp * SPAN:(sp + 1) * SPAN]), 16)
                p.wait(nc.vector, "t")
                p.barrier([nc.vector])
                p.inc("v", nc.vector.scalar_tensor_tensor(
                    otmp[:], ppb[:, 0:512], 1.0, saccsp[:, 0:512], ALU.mult, ALU.add))
                p.wait(nc.vector, "v")
                p.inc("v", nc.vector.tensor_tensor(otmp[:], otmp[:], ppg[:], ALU.mult))
                p.wait(nc.vector, "v")
                p.inc("v", nc.vector.tensor_copy(ybf[:], otmp[:]))
                p.wait(nc.tensor, "v")
                tb = (md % 2) * 1024
                for j in range(JT):
                    p.inc("t", nc.tensor.transpose(
                        ppt[:, tb + j * 128: tb + (j + 1) * 128],
                        ybf[:, j * 128:(j + 1) * 128], ident[:]))
                p.wait(nc.vector, "t")
                for j in range(JT):
                    p.inc("v", nc.vector.scalar_tensor_tensor(
                        s2sp[:, j * D + md * 128: j * D + (md + 1) * 128],
                        ppt[:, tb + j * 128: tb + (j + 1) * 128], 1.0,
                        xsp[:, j * D + md * 128: j * D + (md + 1) * 128],
                        ALU.mult, ALU.add))
            p.wait(nc.sync, "v")
            p.inc("ds", nc.sync.dma_start(
                s2_o.ap().rearrange("(i p) m -> p i m", p=128)[:, JT * sp:JT * (sp + 1), :],
                s2sp.ap().rearrange("p (i m) -> p i m", i=JT)), 16)

            PW = SPAN + 2
            for j in range(JT):
                emit_ln(p, f"h{sp}_{j}", s2sp[:, j * D:(j + 1) * D],
                        xln[:, j * D:(j + 1) * D], lnsq)
            p.wait(nc.tensor, "v")
            for j in range(JT):
                for c in range(NC8):
                    slot = (c % 2) * 1024
                    p.inc("t", nc.tensor.transpose(
                        ppt[:, slot:slot + 128],
                        xln[:, j * D + c * 128: j * D + (c + 1) * 128], ident[:]))
                    p.wait(nc.vector, "t")
                    p.inc("v", nc.vector.tensor_copy(
                        xe[:, c * SPAN + j * 128: c * SPAN + (j + 1) * 128],
                        ppt[:, slot:slot + 128]))
                    p.wait(nc.tensor, "v", p.val("v") - 1)
            for c in range(NC8):
                p.inc("v", nc.vector.tensor_copy(
                    u0p[:, c * PW: c * PW + 2], u0p[:, c * PW + SPAN: c * PW + SPAN + 2]))
                p.inc("v", nc.vector.tensor_copy(
                    u2p[:, c * PW: c * PW + 2], u2p[:, c * PW + SPAN: c * PW + SPAN + 2]))
            p.wait(nc.vector, "v")
            p.wait(nc.tensor, "v")
            for st_, (wmat, updst) in enumerate([(w1e, u0p), (w2e, u2p)]):
                for mc in range(NC8):
                    slot = (mc % 2) * 512
                    if not (st_ == 0 and mc == 0):
                        p.wait(nc.tensor, "v")
                    for k in range(NC8):
                        p.inc("t", nc.tensor.matmul(
                            ppb[:, slot:slot + 512],
                            wmat[:, k * D + mc * 128: k * D + (mc + 1) * 128],
                            xe[:, k * SPAN:(k + 1) * SPAN],
                            start=(k == 0), stop=(k == NC8 - 1)))
                    p.wait(nc.vector, "t")
                    p.inc("v", nc.vector.tensor_copy(
                        updst[:, mc * PW + 2: mc * PW + 2 + SPAN], ppb[:, slot:slot + 512]))
            for c in range(NC8):
                for st_, (up, cu) in enumerate([(u0p, cu0), (u2p, cu2)]):
                    b0 = c * 8 + 4 * st_
                    p.wait(nc.vector, "v")
                    p.inc("v", nc.vector.tensor_scalar_mul(
                        otmp[:], up[:, c * PW: c * PW + SPAN], wcv[:, b0:b0 + 1]))
                    p.wait(nc.vector, "v")
                    p.inc("v", nc.vector.scalar_tensor_tensor(
                        otmp[:], up[:, c * PW + 1: c * PW + 1 + SPAN],
                        wcv[:, b0 + 1:b0 + 2], otmp[:], ALU.mult, ALU.add))
                    p.wait(nc.vector, "v")
                    p.inc("v", nc.vector.scalar_tensor_tensor(
                        otmp[:], up[:, c * PW + 2: c * PW + 2 + SPAN],
                        wcv[:, b0 + 2:b0 + 3], otmp[:], ALU.mult, ALU.add))
                    p.wait(nc.vector, "v")
                    p.inc("v", nc.vector.tensor_scalar_add(
                        cu[:, c * SPAN:(c + 1) * SPAN], otmp[:], wcv[:, b0 + 3:b0 + 4]))
                p.wait(nc.vector, "v")
                p.inc("v", nc.vector.tensor_tensor(
                    ybf[:], cu0[:, c * SPAN:(c + 1) * SPAN],
                    cu2[:, c * SPAN:(c + 1) * SPAN], ALU.mult))
                p.wait(nc.tensor, "v")
                tb = (c % 2) * 1024
                for j in range(JT):
                    p.inc("t", nc.tensor.transpose(
                        ppt[:, tb + j * 128: tb + (j + 1) * 128],
                        ybf[:, j * 128:(j + 1) * 128], ident[:]))
                p.wait(nc.vector, "t")
                for j in range(JT):
                    p.inc("v", nc.vector.tensor_copy(
                        vtm[:, j * D + c * 128: j * D + (c + 1) * 128],
                        ppt[:, tb + j * 128: tb + (j + 1) * 128]))
            p.wait(nc.sync, "v")
            p.inc("ds", nc.sync.dma_start(
                v_o.ap().rearrange("(i p) m -> p i m", p=128)[:, JT * sp:JT * (sp + 1), :],
                vtm.ap().rearrange("p (i m) -> p i m", i=JT)), 16)
            p.inc("ds", nc.sync.dma_start(
                s0_o.ap().rearrange("(c p) t -> p c t", p=128)[:, :, sp * SPAN:(sp + 1) * SPAN],
                cu0.ap().rearrange("p (c t) -> p c t", c=NC8)), 16)
            if sp == 0:
                for c in range(NC8):
                    p.inc("ds", nc.sync.dma_start(
                        uh_o.ap()[:, c * 2: c * 2 + 2], u0p[:, c * PW + 2: c * PW + 4]), 16)
                    p.inc("ds", nc.sync.dma_start(
                        uh_o.ap()[:, 32 + c * 2: 32 + c * 2 + 2], u2p[:, c * PW + 2: c * PW + 4]), 16)
            if sp == NSPAN - 1:
                for c in range(NC8):
                    p.inc("ds", nc.sync.dma_start(
                        uh_o.ap()[:, 16 + c * 2: 16 + c * 2 + 2],
                        u0p[:, c * PW + SPAN: c * PW + SPAN + 2]), 16)
                    p.inc("ds", nc.sync.dma_start(
                        uh_o.ap()[:, 48 + c * 2: 48 + c * 2 + 2],
                        u2p[:, c * PW + SPAN: c * PW + SPAN + 2]), 16)
            p.barrier()
        import os as _os
        if _os.environ.get("KDBG"):
            sacc_o = p.dout("d3_sacc", (D, T), F32)
            p.inc("ds", nc.sync.dma_start(sacc_o.ap(), sacc_d.ap()), 16)
            for nm, t_ in [("d3_gm8", gm8), ("d3_prep", prep), ("d3_pref", pref),
                           ("d3_c8sb", c8sb), ("d3_xe", xe), ("d3_h1b", h1b),
                           ("d3_saccsp", saccsp)]:
                o_ = p.dout(nm, tuple(t_.ap().shape), t_.ap().dtype)
                p.inc("ds", nc.sync.dma_start(o_.ap(), t_[:]), 16)
        p.barrier()
    return p



def build_L5():
    p = Prog()
    nc = p.nc
    SPAN = 512
    NSPAN = T // SPAN
    JT = SPAN // 128
    E = 8

    s3_in = p.din("s3", (T, D), F32)
    idx_in = p.din("idxf_i", (NT, 128), F32)
    gate_in = p.din("gate_i", (NT, 128), F32)
    selpos_in = p.din("selpos_i", (NT, 128), F32)
    allcnt_in = p.din("allcnt", (8, 8), F32)
    cmask_in = p.din("cmask", (8, 1), F32)
    w1_in = p.din("w1", (E, D, D), F32)
    b1_in = p.din("b1", (E, D), F32)
    w2_in = p.din("w2", (E, D, D), F32)
    b2_in = p.din("b2", (E, D), F32)
    ident_in = p.din("identin", (128, 128), BF16)
    iota8_in = p.din("iota8in", (128, 8), F32)

    out_o = p.dout("out", (T, D), F32)
    c8_d = p.dscratch("c8d", (8, D), F32)
    xlnt_d = p.dscratch("xlntd", (D, T), BF16)
    sacc_d = p.dscratch("saccd", (D, T), F32)

    with p.stack:
        for s in ("t", "v", "a", "ds", "dg"):
            p.sem(s)

        ident = p.sb("ident", (128, 128), BF16)
        iota8 = p.sb("iota8", (128, 8), F32)
        cst = p.sb("cst", (128, 2), F32)
        nc.vector.memset(cst[:, 0:1], 1e-5)
        p.inc("v", nc.vector.memset(cst[:, 1:2], -1.0))
        p.cst_eps = cst[:, 0:1]
        p.cst_neg1 = cst[:, 1:2]
        ones_bf = p.sb("ones_bf", (1, SPAN), BF16)
        onescol = p.sb("onescol", (8, 1), BF16)
        onef = p.sb("onef", (1, 1), BF16)
        p.inc("v", nc.vector.memset(ones_bf[:], 1.0))
        p.inc("v", nc.vector.memset(onescol[:], 1.0))
        p.inc("v", nc.vector.memset(onef[:], 1.0))
        p.inc("ds", nc.sync.dma_start(ident[:], ident_in.ap()), 16)
        p.inc("ds", nc.sync.dma_start(iota8[:], iota8_in.ap()), 16)

        acnt = p.sb("acnt", (8, 8), F32)
        cmsk = p.sb("cmsk", (8, 1), F32)
        pref = p.sb("pref", (1, 8), F32)
        prep = p.sb("prep", (128, 8), F32)
        onesc128 = p.sb("onesc128", (1, 128), F32)
        p.inc("v", nc.vector.memset(onesc128[:], 1.0))
        p.inc("ds", nc.sync.dma_start(acnt[:], allcnt_in.ap()), 16)
        p.inc("ds", nc.sync.dma_start(cmsk[:], cmask_in.ap()), 16)
        ppr = p.ps("ppr", (128, 512), F32)
        p.barrier()
        p.inc("t", nc.tensor.matmul(ppr[0:1, 0:8], cmsk[:], acnt[:], start=True, stop=True))
        p.wait(nc.vector, "t")
        p.inc("v", nc.vector.tensor_copy(pref[:], ppr[0:1, 0:8]))
        p.wait(nc.tensor, "v")
        p.inc("t", nc.tensor.matmul(ppr[:, 0:8], onesc128[:], pref[:], start=True, stop=True))
        p.wait(nc.vector, "t")
        p.inc("v", nc.vector.tensor_copy(prep[:], ppr[:, 0:8]))
        p.barrier()

        xsp = p.sb("xsp", (128, JT * D), F32)
        lnsq = p.sb("lnsq", (128, D), F32)
        xlsp = p.sb("xlsp", (128, NC8 * SPAN), BF16)
        xln = p.sb("xln", (128, JT * D), BF16)
        rcol = p.sb("rcol", (128, 8), F32)
        oh = p.sb("oh", (128, 8), F32)
        ohg = p.sb("ohg", (128, 8), BF16)
        gm8 = p.sb("gm8", (8, T), BF16)
        gkrow = p.sb("gkrow", (1, T), BF16)
        ppt = p.ps("ppt", (128, 2048), BF16)

        for sp in range(NSPAN):
            p.wait(nc.sync, "v")
            p.wait(nc.sync, "a")
            p.inc("ds", nc.sync.dma_start(
                xsp.ap().rearrange("p (i m) -> p i m", i=JT),
                s3_in.ap().rearrange("(i p) m -> p i m", p=128)[:, JT * sp:JT * (sp + 1), :]), 16)
            p.barrier([nc.vector, nc.scalar])
            for j in range(JT):
                ti = sp * JT + j
                emit_ln(p, f"f{sp}_{j}", xsp[:, j * D:(j + 1) * D],
                        xln[:, j * D:(j + 1) * D], lnsq)
                p.wait(nc.sync, "v")
                p.inc("ds", nc.sync.dma_start(rcol[:, 0:1], idx_in.ap()[ti:ti + 1, :]), 16)
                p.inc("ds", nc.sync.dma_start(rcol[:, 1:2], gate_in.ap()[ti:ti + 1, :]), 16)
                p.inc("ds", nc.sync.dma_start(rcol[:, 2:3], selpos_in.ap()[ti:ti + 1, :]), 16)
                p.barrier([nc.vector])
                p.inc("v", nc.vector.tensor_tensor(
                    oh[:], rcol[:, 0:1].broadcast_to([128, 8]), iota8[:], ALU.is_equal))
                p.wait(nc.vector, "v")
                p.inc("v", nc.vector.scalar_tensor_tensor(
                    lnsq[:, 0:8], oh[:], 1.0, prep[:], ALU.mult, ALU.mult))
                p.wait(nc.vector, "v")
                p.inc("v", nc.vector.tensor_reduce(rcol[:, 3:4], lnsq[:, 0:8], AX.X, ALU.add))
                p.wait(nc.vector, "v")
                p.inc("v", nc.vector.tensor_tensor(rcol[:, 4:5], rcol[:, 2:3], rcol[:, 3:4], ALU.add))
                p.wait(nc.vector, "v")
                p.inc("v", nc.vector.tensor_single_scalar(rcol[:, 5:6], rcol[:, 4:5], 4096.0, ALU.is_lt))
                p.wait(nc.vector, "v")
                p.inc("v", nc.vector.tensor_tensor(rcol[:, 6:7], rcol[:, 1:2], rcol[:, 5:6], ALU.mult))
                p.wait(nc.vector, "v")
                p.inc("v", nc.vector.tensor_scalar_mul(ohg[:], oh[:], rcol[:, 6:7]))
                p.wait(nc.tensor, "v")
                p.inc("t", nc.tensor.transpose(ppt[0:8, 0:128], ohg[:], ident[:]))
                p.wait(nc.vector, "t")
                p.inc("v", nc.vector.tensor_copy(gm8[:, ti * 128:(ti + 1) * 128], ppt[0:8, 0:128]))
                p.wait(nc.tensor, "v")
                for c in range(NC8):
                    slot = (c % 2) * 1024
                    p.inc("t", nc.tensor.transpose(
                        ppt[:, slot:slot + 128],
                        xln[:, j * D + c * 128: j * D + (c + 1) * 128], ident[:]))
                    p.wait(nc.vector, "t")
                    p.inc("v", nc.vector.tensor_copy(
                        xlsp[:, c * SPAN + j * 128: c * SPAN + (j + 1) * 128],
                        ppt[:, slot:slot + 128]))
                    if c < NC8 - 1:
                        p.wait(nc.tensor, "v", p.val("v") - 1)
            p.wait(nc.sync, "v")
            p.inc("ds", nc.sync.dma_start(
                xlnt_d.ap().rearrange("(c p) t -> p c t", p=128)[:, :, sp * SPAN:(sp + 1) * SPAN],
                xlsp.ap().rearrange("p (c t) -> p c t", c=NC8)), 16)
        p.barrier([nc.tensor])
        for sp in range(NSPAN):
            p.inc("t", nc.tensor.matmul(ppr[0:1, 0:SPAN], onescol[:],
                                        gm8[:, sp * SPAN:(sp + 1) * SPAN],
                                        start=True, stop=True))
            p.wait(nc.vector, "t")
            p.inc("v", nc.vector.tensor_copy(gkrow[:, sp * SPAN:(sp + 1) * SPAN],
                                             ppr[0:1, 0:SPAN]))
            p.wait(nc.tensor, "v")
        p.barrier()

        w1e = p.sb("w1e", (128, 2 * NC8 * D), BF16)
        w2e = p.sb("w2e", (128, 2 * NC8 * D), BF16)
        b1r = p.sb("b1r", (1, 2 * D), BF16)
        b2r = p.sb("b2r", (1, 2 * D), BF16)
        rb1 = p.sb("rb1", (128, 2 * NC8), F32)
        rb1b = p.sb("rb1b", (128, 2 * NC8), BF16)
        seltab = p.sb("seltab", (8, 8 * 128), BF16)
        p.wait(nc.vector, "ds")
        for e_ in range(8):
            p.inc("v", nc.vector.tensor_copy(
                seltab[:, e_ * 128:(e_ + 1) * 128],
                ident[0:8, e_:e_ + 1].broadcast_to([8, 128])))
            p.wait(nc.vector, "v")
        xe = p.sb("xe", (128, NC8 * SPAN), BF16)
        h1b = p.sb("h1b", (128, NC8 * SPAN), BF16)
        saccsp = p.sb("saccsp", (128, 4 * SPAN), F32)
        aidx, amk = 0, {}
        ppb = p.ps("ppb", (128, 2048), F32)
        ppg = p.ps("ppg", (128, 512), F32)

        def load_expert(e, sl):
            WO = sl * NC8 * D
            p.inc("dg", nc.gpsimd.dma_start(
                w1e.ap()[:, WO:WO + NC8 * D].rearrange("p (k m) -> p k m", k=NC8),
                w1_in.ap()[e].rearrange("(k p) m -> p k m", p=128)), 16)
            p.inc("dg", nc.gpsimd.dma_start(
                w2e.ap()[:, WO:WO + NC8 * D].rearrange("p (k m) -> p k m", k=NC8),
                w2_in.ap()[e].rearrange("(k p) m -> p k m", p=128)), 16)
            p.inc("dg", nc.gpsimd.dma_start(b1r[0:1, sl * D:(sl + 1) * D],
                                            b1_in.ap()[e].unsqueeze(0)), 16)
            p.inc("dg", nc.gpsimd.dma_start(b2r[0:1, sl * D:(sl + 1) * D],
                                            b2_in.ap()[e].unsqueeze(0)), 16)
            for c in range(NC8):
                p.inc("ds", nc.sync.dma_start(
                    rb1[:, sl * NC8 + c: sl * NC8 + c + 1],
                    b1_in.ap()[e, c * 128:(c + 1) * 128].unsqueeze(-1)), 16)
            return (p.val("dg"), p.val("ds"))

        wmarks = {0: load_expert(0, 0)}
        for e in range(E):
            sl = e % 2
            WO = sl * NC8 * D
            gmk, smk = wmarks[e]
            for eng in (nc.tensor, nc.vector):
                p.wait(eng, "dg", gmk)
                p.wait(eng, "ds", smk)
            if e + 1 < E:
                wmarks[e + 1] = load_expert(e + 1, 1 - sl)
            p.inc("v", nc.vector.tensor_scalar_max(
                rb1[:, sl * NC8:(sl + 1) * NC8], rb1[:, sl * NC8:(sl + 1) * NC8], 0.0))
            p.wait(nc.vector, "v")
            p.inc("v", nc.vector.tensor_copy(rb1b[:, sl * NC8:(sl + 1) * NC8],
                                             rb1[:, sl * NC8:(sl + 1) * NC8]))
            p.barrier([nc.tensor])
            for half in range(2):
                for k in range(NC8):
                    p.inc("t", nc.tensor.matmul(
                        ppb[0:1, half * 512:half * 512 + 512],
                        rb1b[:, sl * NC8 + k: sl * NC8 + k + 1],
                        w2e[:, WO + k * D + half * 512: WO + k * D + half * 512 + 512],
                        start=(k == 0), stop=False))
                p.inc("t", nc.tensor.matmul(
                    ppb[0:1, half * 512:half * 512 + 512],
                    onef[:], b2r[0:1, sl * D + half * 512: sl * D + half * 512 + 512],
                    start=False, stop=True))
            p.wait(nc.vector, "t")
            p.inc("v", nc.vector.tensor_copy(lnsq[0:1, 0:D], ppb[0:1, 0:1024]))
            p.wait(nc.sync, "v")
            p.inc("ds", nc.sync.dma_start(c8_d.ap()[e:e + 1, :], lnsq[0:1, 0:D]), 16)

            for sp in range(NSPAN):
                p.wait(nc.tensor, "v")
                p.barrier([nc.sync])
                p.inc("ds", nc.sync.dma_start(
                    xlsp.ap().rearrange("p (c t) -> p c t", c=NC8),
                    xlnt_d.ap().rearrange("(c p) t -> p c t", p=128)[:, :, sp * SPAN:(sp + 1) * SPAN]), 16)
                p.inc("t", nc.tensor.matmul(
                    ppg[:], seltab[:, e * 128:(e + 1) * 128],
                    gm8[:, sp * SPAN:(sp + 1) * SPAN],
                    start=True, stop=True))
                p.wait(nc.vector, "t")
                p.barrier([nc.vector])
                for c in range(NC8):
                    p.inc("v", nc.vector.tensor_tensor(
                        xe[:, c * SPAN:(c + 1) * SPAN],
                        xlsp[:, c * SPAN:(c + 1) * SPAN],
                        ppg[:], ALU.mult))
                p.wait(nc.tensor, "v")
                evk = {}
                for mf in range(NC8):
                    slot = (mf % 4) * 512
                    if mf == 4:
                        p.wait(nc.tensor, "v", evk[3])
                    for k in range(NC8):
                        p.inc("t", nc.tensor.matmul(
                            ppb[:, slot:slot + 512],
                            w1e[:, WO + k * D + mf * 128: WO + k * D + (mf + 1) * 128],
                            xe[:, k * SPAN:(k + 1) * SPAN],
                            start=(k == 0), stop=False))
                    p.inc("t", nc.tensor.matmul(
                        ppb[:, slot:slot + 512],
                        b1r[0:1, sl * D + mf * 128: sl * D + (mf + 1) * 128], ones_bf[:],
                        start=False, stop=True))
                    p.wait(nc.vector, "t")
                    evk[mf] = p.inc("v", nc.vector.scalar_tensor_tensor(
                        h1b[:, mf * SPAN:(mf + 1) * SPAN],
                        ppb[:, slot:slot + 512], 0.0,
                        rb1[:, sl * NC8 + mf: sl * NC8 + mf + 1].broadcast_to([128, SPAN]),
                        ALU.max, ALU.subtract))
                p.wait(nc.tensor, "v")
                cpk = {}
                for md in range(NC8):
                    slot = (md % 4) * 512
                    if md == 4:
                        p.wait(nc.tensor, "v", cpk[3])
                    for k in range(NC8):
                        p.inc("t", nc.tensor.matmul(
                            ppb[:, slot:slot + 512],
                            w2e[:, WO + k * D + md * 128: WO + k * D + (md + 1) * 128],
                            h1b[:, k * SPAN:(k + 1) * SPAN],
                            start=(k == 0), stop=(k == NC8 - 1)))
                    p.wait(nc.vector, "t")
                    sl4 = (aidx % 4) * 512
                    if aidx >= 4:
                        p.wait(nc.vector, "dg", amk[aidx - 4])
                    cpk[md] = p.inc("v", nc.vector.tensor_copy(
                        saccsp[:, sl4:sl4 + 512], ppb[:, slot:slot + 512]))
                    p.wait(nc.gpsimd, "v")
                    dstap = sacc_d.ap().rearrange("(c p) t -> p c t", p=128)[:, md, sp * SPAN:(sp + 1) * SPAN]
                    if e == 0:
                        amk[aidx] = p.inc("dg", nc.gpsimd.dma_start(dstap, saccsp[:, sl4:sl4 + 512]), 16)
                    else:
                        amk[aidx] = p.inc("dg", nc.gpsimd.dma_start(dstap, saccsp[:, sl4:sl4 + 512], accum_op=ALU.add), 16)
                    aidx += 1
            p.barrier()

        c8sb = p.sb("c8sb", (8, D), BF16)
        osp = p.sb("osp", (128, JT * D), F32)
        otmp = p.sb("otmp", (128, SPAN), F32)
        ybf = p.sb("ybf", (128, SPAN), BF16)
        p.inc("dg", nc.gpsimd.dma_start(c8sb[:], c8_d.ap()), 16)
        p.barrier()
        for sp in range(NSPAN):
            p.wait(nc.sync, "v")
            p.inc("ds", nc.sync.dma_start(
                xsp.ap().rearrange("p (i m) -> p i m", i=JT),
                s3_in.ap().rearrange("(i p) m -> p i m", p=128)[:, JT * sp:JT * (sp + 1), :]), 16)
            p.wait(nc.tensor, "v")
            p.inc("t", nc.tensor.matmul(
                ppg[:], ones_bf[0:1, 0:128],
                gkrow[:, sp * SPAN:(sp + 1) * SPAN], start=True, stop=True))
            p.barrier([nc.vector])
            for md in range(NC8):
                p.inc("t", nc.tensor.matmul(
                    ppb[:, 0:512], c8sb[:, md * 128:(md + 1) * 128],
                    gm8[:, sp * SPAN:(sp + 1) * SPAN], start=True, stop=True))
                p.wait(nc.sync, "v")
                p.inc("ds", nc.sync.dma_start(
                    saccsp[:, 0:512],
                    sacc_d.ap().rearrange("(c p) t -> p c t", p=128)[:, md, sp * SPAN:(sp + 1) * SPAN]), 16)
                p.wait(nc.vector, "t")
                p.barrier([nc.vector])
                p.inc("v", nc.vector.scalar_tensor_tensor(
                    otmp[:], ppb[:, 0:512], 1.0, saccsp[:, 0:512], ALU.mult, ALU.add))
                p.wait(nc.vector, "v")
                p.inc("v", nc.vector.tensor_tensor(otmp[:], otmp[:], ppg[:], ALU.mult))
                p.wait(nc.vector, "v")
                p.inc("v", nc.vector.tensor_copy(ybf[:], otmp[:]))
                p.wait(nc.tensor, "v")
                tb = (md % 2) * 1024
                for j in range(JT):
                    p.inc("t", nc.tensor.transpose(
                        ppt[:, tb + j * 128: tb + (j + 1) * 128],
                        ybf[:, j * 128:(j + 1) * 128], ident[:]))
                p.wait(nc.vector, "t")
                for j in range(JT):
                    p.inc("v", nc.vector.scalar_tensor_tensor(
                        osp[:, j * D + md * 128: j * D + (md + 1) * 128],
                        ppt[:, tb + j * 128: tb + (j + 1) * 128], 1.0,
                        xsp[:, j * D + md * 128: j * D + (md + 1) * 128],
                        ALU.mult, ALU.add))
            p.wait(nc.sync, "v")
            p.inc("ds", nc.sync.dma_start(
                out_o.ap().rearrange("(i p) m -> p i m", p=128)[:, JT * sp:JT * (sp + 1), :],
                osp.ap().rearrange("p (i m) -> p i m", i=JT)), 16)
        p.barrier()
    return p



def build_L4():
    p = Prog()
    nc = p.nc
    SPAN = 512
    NSPAN = T // SPAN
    JT = SPAN // 128
    FW = 64

    s2_in = p.din("s2", (T, D), F32)
    vf_in = p.din("vf", (T, D), BF16)
    vh_in = p.din("vhalo", (1024, D), BF16)
    s0_in = p.din("s0ct", (D, T), BF16)
    uh_in = p.din("uht", (128, 64), BF16)
    uhp_in = p.din("uhtp", (128, 64), BF16)
    b1f_in = p.din("b1f", (FW,), F32)
    w2f_in = p.din("w2f", (FW, FW), F32)
    b2f_in = p.din("b2f", (FW,), F32)
    w3f_in = p.din("w3f", (FW, D), F32)
    fb_in = p.din("fb", (D,), F32)
    wo_in = p.din("wo", (D, D), F32)
    bo_in = p.din("bo", (D,), F32)
    wconv_in = p.din("wconv", (3072, 1, 3), F32)
    bconv_in = p.din("bconv", (3072,), F32)
    bin_in = p.din("binh", (3072,), F32)
    binsel_in = p.din("binsel", (3072,), F32)
    wr_in = p.din("wrin", (D, 8), F32)
    br_in = p.din("brin", (8,), F32)
    ident_in = p.din("identin", (128, 128), BF16)
    ut_in = p.din("utincin", (128, 128), BF16)
    iota8_in = p.din("iota8in", (128, 8), F32)
    hgi_in = p.din("hgidx", (128, 9), I32)

    s3_o = p.dout("s3", (T, D), F32)
    idx_o = p.dout("idxf2_o", (NT, 128), F32)
    gate_o = p.dout("gate2_o", (NT, 128), F32)
    selpos_o = p.dout("selpos2_o", (NT, 128), F32)
    cnt_o = p.dout("cnt2_o", (1, 8), F32)
    hpad_d = p.dscratch("hpadd", (1504, 1), BF16)
    vp_d = p.dscratch("vpd", (2, D), BF16)

    with p.stack:
        for s in ("t", "v", "a", "ds", "dg"):
            p.sem(s)

        ident = p.sb("ident", (128, 128), BF16)
        utinc = p.sb("utinc", (128, 128), BF16)
        iota8 = p.sb("iota8", (128, 8), F32)
        cst = p.sb("cst", (128, 2), F32)
        nc.vector.memset(cst[:, 0:1], 1e-5)
        p.inc("v", nc.vector.memset(cst[:, 1:2], -1.0))
        p.cst_eps = cst[:, 0:1]
        p.cst_neg1 = cst[:, 1:2]
        ones_bf = p.sb("ones_bf", (1, SPAN), BF16)
        onescol = p.sb("onescol", (128, 1), BF16)
        onef32 = p.sb("onef32", (1, 1), F32)
        p.inc("v", nc.vector.memset(ones_bf[:], 1.0))
        p.inc("v", nc.vector.memset(onescol[:], 1.0))
        p.inc("v", nc.vector.memset(onef32[:], 1.0))

        vres = p.sb("vres", (128, NT * D), BF16)
        vhl = p.sb("vhl", (128, 8 * D), BF16)
        wosb = p.sb("wosb", (128, NC8 * D), BF16)
        fbrep = p.sb("fbrep", (128, D), BF16)
        borow = p.sb("borow", (1, D), BF16)
        wr = p.sb("wr", (128, NC8 * 8), BF16)
        brr = p.sb("brr", (1, 8), BF16)
        p.inc("ds", nc.sync.dma_start(
            vres.ap().rearrange("p (g m) -> p g m", g=NT),
            vf_in.ap().rearrange("(g p) m -> p g m", p=128)), 16)
        p.inc("ds", nc.sync.dma_start(
            vhl.ap().rearrange("p (g m) -> p g m", g=8),
            vh_in.ap().rearrange("(g p) m -> p g m", p=128)), 16)
        p.inc("dg", nc.gpsimd.dma_start(
            wosb.ap().rearrange("p (k m) -> p k m", k=NC8),
            wo_in.ap().rearrange("(k p) m -> p k m", p=128)), 16)
        p.inc("dg", nc.gpsimd.dma_start(
            fbrep[:], fb_in.ap().unsqueeze(0).broadcast_to([128, D])), 16)
        p.inc("dg", nc.gpsimd.dma_start(borow[:], bo_in.ap().unsqueeze(0)), 16)
        p.inc("dg", nc.gpsimd.dma_start(
            wr.ap().rearrange("p (k m) -> p k m", k=NC8),
            wr_in.ap().rearrange("(k p) m -> p k m", p=128)), 16)
        p.inc("dg", nc.gpsimd.dma_start(brr[:], br_in.ap().unsqueeze(0)), 16)
        p.inc("ds", nc.sync.dma_start(ident[:], ident_in.ap()), 16)
        p.inc("ds", nc.sync.dma_start(utinc[:], ut_in.ap()), 16)
        p.inc("ds", nc.sync.dma_start(iota8[:], iota8_in.ap()), 16)

        b1c = p.sb("b1c", (FW, 1), F32)
        w2sb = p.sb("w2sb", (FW, FW), F32)
        b2fr = p.sb("b2fr", (1, FW), F32)
        w3sb = p.sb("w3sb", (FW, D), F32)
        fcol = p.sb("fcol", (FW, 8), F32)
        hrow = p.sb("hrow", (1, D), F32)
        hpz = p.sb("hpz", (1, 1504), BF16)
        tix = p.sb("tix", (128, 9), I32)
        Hh = p.sb("Hh", (128, 9 * 128), BF16)
        p.inc("ds", nc.sync.dma_start(b1c[:], b1f_in.ap().unsqueeze(-1)), 16)
        p.inc("ds", nc.sync.dma_start(w2sb[:], w2f_in.ap()), 16)
        p.inc("ds", nc.sync.dma_start(b2fr[:], b2f_in.ap().unsqueeze(0)), 16)
        p.inc("ds", nc.sync.dma_start(w3sb[:], w3f_in.ap()), 16)
        p.inc("ds", nc.sync.dma_start(tix[:], hgi_in.ap()), 16)
        p.inc("v", nc.vector.memset(hpz[:], 0.0))
        ppr = p.ps("ppr", (128, 512), F32)
        p.barrier()

        def emit_silu(zcol, outcol):
            p.wait(nc.scalar, "v")
            p.inc("a", nc.scalar.activation(fcol[:, 0:1], zcol, AF.Exp,
                                            scale=cst[0:FW, 1:2]))
            p.wait(nc.vector, "a")
            p.inc("v", nc.vector.tensor_single_scalar(fcol[:, 1:2], fcol[:, 0:1], 1.0, ALU.add))
            p.wait(nc.vector, "v")
            p.inc("v", nc.vector.reciprocal(fcol[:, 2:3], fcol[:, 1:2]))
            p.wait(nc.vector, "v")
            p.inc("v", nc.vector.tensor_tensor(outcol, zcol, fcol[:, 2:3], ALU.mult))
            p.wait(nc.vector, "v")

        emit_silu(b1c[:], fcol[:, 3:4])
        p.wait(nc.tensor, "v")
        p.inc("t", nc.tensor.matmul(ppr[0:FW, 0:1], w2sb[:], fcol[:, 3:4],
                                    start=True, stop=False))
        p.inc("t", nc.tensor.matmul(ppr[0:FW, 0:1], b2fr[:], onef32[:],
                                    start=False, stop=True))
        p.wait(nc.vector, "t")
        p.inc("v", nc.vector.tensor_copy(fcol[:, 4:5], ppr[0:FW, 0:1]))
        p.wait(nc.vector, "v")
        emit_silu(fcol[:, 4:5], fcol[:, 5:6])
        for half in range(2):
            p.wait(nc.tensor, "v")
            p.inc("t", nc.tensor.matmul(ppr[0:1, 0:512], fcol[:, 5:6],
                                        w3sb[:, half * 512:(half + 1) * 512],
                                        start=True, stop=True))
            p.wait(nc.vector, "t")
            p.inc("v", nc.vector.tensor_copy(hrow[:, half * 512:(half + 1) * 512],
                                             ppr[0:1, 0:512]))
            p.wait(nc.tensor, "v")
        p.wait(nc.vector, "v")
        p.inc("v", nc.vector.tensor_copy(hpz[:, 127:1151], hrow[:]))
        p.wait(nc.sync, "v")
        p.inc("ds", nc.sync.dma_start(
            hpad_d.ap().rearrange("(a b) c -> a (b c)", a=1), hpz[:]), 16)
        p.barrier()
        for dlt in range(9):
            p.inc("dg", nc.gpsimd.indirect_dma_start(
                Hh[:, dlt * 128:(dlt + 1) * 128], None, hpad_d.ap(),
                bass.IndirectOffsetOnAxis(ap=tix[:, dlt:dlt + 1], axis=0)), 16)

        wcv = p.sb("wcv", (128, 8 * NC8), F32)
        bic = p.sb("bic", (128, 2 * NC8), F32)
        bicsel = p.sb("bicsel", (128, 2 * NC8), F32)
        uho = p.sb("uho", (128, 64), BF16)
        uhp = p.sb("uhp", (128, 64), BF16)
        pc = p.sb("pc", (128, 8), F32)
        pcu0 = p.sb("pcu0", (128, 16), F32)
        pcu2 = p.sb("pcu2", (128, 16), F32)
        pvb = p.sb("pvb", (128, 16), BF16)
        ps0b = p.sb("ps0b", (128, 16), BF16)
        with p.nc.allow_non_contiguous_dma(reason="tiny strided w_conv columns"):
            for c in range(NC8):
                for j in range(3):
                    p.inc("ds", nc.sync.dma_start(
                        wcv[:, c * 8 + j: c * 8 + j + 1],
                        wconv_in.ap()[c * 128:(c + 1) * 128, 0, j:j + 1]), 16)
                    p.inc("ds", nc.sync.dma_start(
                        wcv[:, c * 8 + 4 + j: c * 8 + 5 + j],
                        wconv_in.ap()[2 * D + c * 128: 2 * D + (c + 1) * 128, 0, j:j + 1]), 16)
                p.inc("ds", nc.sync.dma_start(
                    wcv[:, c * 8 + 3: c * 8 + 4],
                    bconv_in.ap()[c * 128:(c + 1) * 128].unsqueeze(-1)), 16)
                p.inc("ds", nc.sync.dma_start(
                    wcv[:, c * 8 + 7: c * 8 + 8],
                    bconv_in.ap()[2 * D + c * 128: 2 * D + (c + 1) * 128].unsqueeze(-1)), 16)
        for c in range(NC8):
            for st_, off in ((0, 0), (1, 2 * D)):
                p.inc("ds", nc.sync.dma_start(
                    bic[:, st_ * NC8 + c: st_ * NC8 + c + 1],
                    bin_in.ap()[off + c * 128: off + (c + 1) * 128].unsqueeze(-1)), 16)
                p.inc("ds", nc.sync.dma_start(
                    bicsel[:, st_ * NC8 + c: st_ * NC8 + c + 1],
                    binsel_in.ap()[off + c * 128: off + (c + 1) * 128].unsqueeze(-1)), 16)
        p.inc("ds", nc.sync.dma_start(uho[:], uh_in.ap()), 16)
        p.inc("ds", nc.sync.dma_start(uhp[:], uhp_in.ap()), 16)
        p.barrier()
        for st_, (hb, tb, tgt) in enumerate([(0, 16, pcu0), (32, 48, pcu2)]):
            for c in range(NC8):
                b0 = c * 8 + 4 * st_
                bcl = bic[:, st_ * NC8 + c: st_ * NC8 + c + 1]
                bsl = bicsel[:, st_ * NC8 + c: st_ * NC8 + c + 1]
                for k, (src, bias) in enumerate([
                        (uhp[:, tb + c * 2: tb + c * 2 + 1], bsl),
                        (uhp[:, tb + c * 2 + 1: tb + c * 2 + 2], bsl),
                        (uho[:, hb + c * 2: hb + c * 2 + 1], bcl),
                        (uho[:, hb + c * 2 + 1: hb + c * 2 + 2], bcl)]):
                    p.inc("v", nc.vector.tensor_copy(pc[:, k:k + 1], src))
                    p.wait(nc.vector, "v")
                    p.inc("v", nc.vector.tensor_tensor(
                        pc[:, k:k + 1], pc[:, k:k + 1], bias, ALU.add))
                    p.wait(nc.vector, "v")
                for t_ in range(2):
                    p.inc("v", nc.vector.tensor_scalar_mul(
                        pc[:, 4:5], pc[:, t_:t_ + 1], wcv[:, b0:b0 + 1]))
                    p.wait(nc.vector, "v")
                    p.inc("v", nc.vector.scalar_tensor_tensor(
                        pc[:, 4:5], pc[:, t_ + 1:t_ + 2], wcv[:, b0 + 1:b0 + 2],
                        pc[:, 4:5], ALU.mult, ALU.add))
                    p.wait(nc.vector, "v")
                    p.inc("v", nc.vector.scalar_tensor_tensor(
                        pc[:, 4:5], pc[:, t_ + 2:t_ + 3], wcv[:, b0 + 2:b0 + 3],
                        pc[:, 4:5], ALU.mult, ALU.add))
                    p.wait(nc.vector, "v")
                    p.inc("v", nc.vector.tensor_scalar_add(
                        tgt[:, c * 2 + t_: c * 2 + t_ + 1], pc[:, 4:5],
                        wcv[:, b0 + 3:b0 + 4]))
                    p.wait(nc.vector, "v")
        p.inc("v", nc.vector.tensor_tensor(pvb[:], pcu0[:], pcu2[:], ALU.mult))
        p.wait(nc.vector, "v")
        p.inc("v", nc.vector.tensor_copy(ps0b[:], pcu0[:]))
        p.wait(nc.sync, "v")
        for c in range(NC8):
            for t_ in range(2):
                p.inc("ds", nc.sync.dma_start(
                    vp_d.ap()[t_:t_ + 1, c * 128:(c + 1) * 128],
                    pvb[:, c * 2 + t_: c * 2 + t_ + 1]), 16)
        p.barrier()
        p.inc("ds", nc.sync.dma_start(vres.ap()[0:2, 0:D], vp_d.ap()), 16)
        p.barrier()

        xsp = p.sb("xsp", (128, JT * D), F32)
        s0sp = p.sb("s0sp", (128, NC8 * SPAN), BF16)
        g3dm = p.sb("g3dm", (128, NC8 * SPAN), BF16)
        vfbs = p.sb("vfbs", (128, D), F32)
        wbf = p.sb("wbf", (128, D), BF16)
        obf = p.sb("obf", (128, SPAN), BF16)
        s3sp = p.sb("s3sp", (128, JT * D), F32)
        xln = p.sb("xln", (128, JT * D), BF16)
        lnsq = p.sb("lnsq", (128, D), F32)
        xtb = p.sb("xtb", (128, 128), BF16)
        lg = p.sb("lg", (128, 8), F32)
        mcol = p.sb("mcol", (128, 8), F32)
        idx8 = p.sb("idx8", (128, 8), mybir.dt.uint32)
        idxf = p.sb("idxf", (128, 1), F32)
        oh = p.sb("oh", (128, 8), F32)
        ohb = p.sb("ohb", (128, 8), BF16)
        posg = p.sb("posg", (128, 8), F32)
        selc = p.sb("selc", (128, 8), F32)
        runr = p.sb("runr", (1, 8), F32)
        runb = p.sb("runb", (1, 8), BF16)
        exps = p.sb("exps", (128, 8), F32)
        ppw = p.ps("ppw", (128, 2048), F32)
        ppo = p.ps("ppo", (128, 512), F32)
        ppt = p.ps("ppt", (128, 2048), BF16)
        p.inc("v", nc.vector.memset(runr[:], 0.0))
        p.inc("v", nc.vector.memset(runb[:], 0.0))
        p.barrier()

        wevmk = {}
        for sp in range(NSPAN):
            p.wait(nc.sync, "v")
            p.wait(nc.sync, "t")
            p.inc("ds", nc.sync.dma_start(
                xsp.ap().rearrange("p (i m) -> p i m", i=JT),
                s2_in.ap().rearrange("(i p) m -> p i m", p=128)[:, JT * sp:JT * (sp + 1), :]), 16)
            p.inc("ds", nc.sync.dma_start(
                s0sp.ap().rearrange("p (c t) -> p c t", c=NC8),
                s0_in.ap().rearrange("(c p) t -> p c t", p=128)[:, :, sp * SPAN:(sp + 1) * SPAN]), 16)
            p.barrier([nc.vector, nc.scalar])
            if sp == 0:
                for c in range(NC8):
                    for t_ in range(2):
                        p.inc("v", nc.vector.tensor_copy(
                            s0sp[:, c * SPAN + t_: c * SPAN + t_ + 1],
                            ps0b[:, c * 2 + t_: c * 2 + t_ + 1]))
                        p.wait(nc.vector, "v")
            for jj in range(JT):
                g = sp * JT + jj
                slot = (g % 2) * 1024
                if g >= 2:
                    p.wait(nc.tensor, "v", wevmk[g - 2])
                for dlt in range(9):
                    s_ = g - dlt
                    vsrc = (vres[:, s_ * D: s_ * D + D] if s_ >= 0
                            else vhl[:, (s_ + 8) * D: (s_ + 9) * D])
                    for dc in range(2):
                        p.inc("t", nc.tensor.matmul(
                            ppw[:, slot + dc * 512: slot + dc * 512 + 512],
                            Hh[:, dlt * 128:(dlt + 1) * 128],
                            vsrc[:, dc * 512:(dc + 1) * 512],
                            start=(dlt == 0), stop=(dlt == 8)))
                p.wait(nc.vector, "t")
                p.inc("v", nc.vector.tensor_tensor(
                    vfbs[:], vres[:, g * D:(g + 1) * D], fbrep[:], ALU.mult))
                p.wait(nc.vector, "v")
                wevmk[g] = p.inc("v", nc.vector.tensor_tensor(
                    wbf[:], ppw[:, slot:slot + 1024], vfbs[:], ALU.add))
                p.wait(nc.tensor, "v")
                for c in range(NC8):
                    tslot = (c % 2) * 1024
                    p.inc("t", nc.tensor.transpose(
                        ppt[:, tslot:tslot + 128], wbf[:, c * 128:(c + 1) * 128],
                        ident[:]))
                    p.wait(nc.vector, "t")
                    p.inc("v", nc.vector.tensor_tensor(
                        g3dm[:, c * SPAN + jj * 128: c * SPAN + (jj + 1) * 128],
                        ppt[:, tslot:tslot + 128],
                        s0sp[:, c * SPAN + jj * 128: c * SPAN + (jj + 1) * 128],
                        ALU.mult))
                    p.wait(nc.tensor, "v", p.val("v") - 1)
            p.wait(nc.tensor, "v")
            for m in range(NC8):
                for k in range(NC8):
                    p.inc("t", nc.tensor.matmul(
                        ppo[:], wosb[:, k * D + m * 128: k * D + (m + 1) * 128],
                        g3dm[:, k * SPAN:(k + 1) * SPAN],
                        start=(k == 0), stop=False))
                p.inc("t", nc.tensor.matmul(
                    ppo[:], borow[:, m * 128:(m + 1) * 128], ones_bf[:],
                    start=False, stop=True))
                p.wait(nc.vector, "t")
                p.inc("v", nc.vector.tensor_copy(obf[:], ppo[:]))
                p.wait(nc.tensor, "v")
                for j in range(JT):
                    tslot = (j % 2) * 1024
                    p.inc("t", nc.tensor.transpose(
                        ppt[:, tslot:tslot + 128], obf[:, j * 128:(j + 1) * 128],
                        ident[:]))
                    p.wait(nc.vector, "t")
                    p.inc("v", nc.vector.scalar_tensor_tensor(
                        s3sp[:, j * D + m * 128: j * D + (m + 1) * 128],
                        ppt[:, tslot:tslot + 128], 1.0,
                        xsp[:, j * D + m * 128: j * D + (m + 1) * 128],
                        ALU.mult, ALU.add))
                    p.wait(nc.tensor, "v", p.val("v") - 1)
            p.wait(nc.sync, "v")
            p.inc("ds", nc.sync.dma_start(
                s3_o.ap().rearrange("(i p) m -> p i m", p=128)[:, JT * sp:JT * (sp + 1), :],
                s3sp.ap().rearrange("p (i m) -> p i m", i=JT)), 16)

            for j in range(JT):
                ti = sp * JT + j
                emit_ln(p, f"q{sp}_{j}", s3sp[:, j * D:(j + 1) * D],
                        xln[:, j * D:(j + 1) * D], lnsq)
                p.wait(nc.tensor, "v")
                for c in range(NC8):
                    p.inc("t", nc.tensor.transpose(
                        ppt[:, 0:128], xln[:, j * D + c * 128: j * D + (c + 1) * 128],
                        ident[:]))
                    p.wait(nc.vector, "t")
                    p.inc("v", nc.vector.tensor_copy(xtb[:], ppt[:, 0:128]))
                    p.wait(nc.tensor, "v")
                    p.inc("t", nc.tensor.matmul(
                        ppr[:, 0:8], xtb[:], wr[:, c * 8:(c + 1) * 8],
                        start=(c == 0), stop=False))
                p.inc("t", nc.tensor.matmul(
                    ppr[:, 0:8], ones_bf[0:1, 0:128], brr[:],
                    start=False, stop=True))
                p.wait(nc.vector, "t")
                p.inc("v", nc.vector.tensor_copy(lg[:], ppr[:, 0:8]))
                p.wait(nc.vector, "v")
                p.inc("v", nc.vector.reduce_max(mcol[:, 0:1], lg[:], AX.X))
                p.wait(nc.vector, "v")
                p.inc("v", nc.vector.tensor_scalar_mul(mcol[:, 1:2], mcol[:, 0:1], -1.0))
                p.wait(nc.scalar, "v")
                p.inc("a", nc.scalar.activation(exps[:], lg[:], AF.Exp,
                                                bias=mcol[:, 1:2], accum_out=mcol[:, 2:3]))
                p.wait(nc.vector, "a")
                p.inc("v", nc.vector.reciprocal(mcol[:, 3:4], mcol[:, 2:3]))
                p.inc("v", nc.vector.max_index(idx8[:], mcol[:, 0:1].broadcast_to([128, 8]), lg[:]))
                p.wait(nc.vector, "v")
                p.inc("v", nc.vector.tensor_copy(idxf[:], idx8[:, 0:1]))
                p.wait(nc.vector, "v")
                p.inc("v", nc.vector.tensor_tensor(
                    oh[:], idxf.ap().broadcast_to([128, 8]), iota8[:], ALU.is_equal))
                p.wait(nc.vector, "v")
                p.inc("v", nc.vector.tensor_copy(ohb[:], oh[:]))
                p.wait(nc.tensor, "v")
                p.inc("t", nc.tensor.matmul(ppr[:, 8:16], utinc[:], ohb[:],
                                            start=True, stop=False))
                p.inc("t", nc.tensor.matmul(ppr[:, 8:16], ones_bf[0:1, 0:128], runb[:],
                                            start=False, stop=True))
                p.inc("t", nc.tensor.matmul(ppr[0:1, 16:24], onescol[:], ohb[:],
                                            start=True, stop=True))
                p.wait(nc.vector, "t")
                p.inc("v", nc.vector.tensor_tensor(posg[:], ppr[:, 8:16], oh[:], ALU.mult))
                p.wait(nc.vector, "v")
                p.inc("v", nc.vector.tensor_reduce(selc[:, 0:1], posg[:], AX.X, ALU.add))
                p.inc("v", nc.vector.tensor_tensor(runr[:], runr[:], ppr[0:1, 16:24], ALU.add))
                p.wait(nc.vector, "v")
                p.inc("v", nc.vector.tensor_copy(runb[:], runr[:]))
                p.wait(nc.sync, "v")
                p.inc("ds", nc.sync.dma_start(idx_o.ap()[ti:ti + 1, :], idxf[:]), 16)
                p.inc("ds", nc.sync.dma_start(gate_o.ap()[ti:ti + 1, :], mcol[:, 3:4]), 16)
                p.inc("ds", nc.sync.dma_start(selpos_o.ap()[ti:ti + 1, :], selc[:, 0:1]), 16)
            p.barrier()

        p.wait(nc.sync, "v")
        p.inc("ds", nc.sync.dma_start(cnt_o.ap(), runr[:]), 16)
        import os as _os
        if _os.environ.get("KDBG"):
            for nm, t_ in [("d4_Hh", Hh), ("d4_hrow", hrow), ("d4_pcu0", pcu0),
                           ("d4_pcu2", pcu2), ("d4_pvb", pvb), ("d4_g3dm", g3dm),
                           ("d4_wbf", wbf), ("d4_s0sp", s0sp), ("d4_fcol", fcol)]:
                o_ = p.dout(nm, tuple(t_.ap().shape), t_.ap().dtype)
                p.inc("ds", nc.sync.dma_start(o_.ap(), t_[:]), 16)
        p.barrier()
    return p



_programs = {}


def _get(name, builder):
    if name not in _programs:
        _programs[name] = builder()
    return _programs[name]


def _shard_x(x):
    x = np.ascontiguousarray(np.asarray(x, np.float32))
    return [np.ascontiguousarray(x[c >> 1, (c & 1) * T:((c & 1) + 1) * T, :])
            for c in range(NCORE)]


_IDENT = np.eye(128, dtype=ml_dtypes.bfloat16)


def run_L1(inputs):
    p = _get("L1", build_L1)
    xs = _shard_x(inputs["L0_attn_a_x"])
    base = {
        "w_in": np.asarray(inputs["L0_attn_a_w_in"], np.float32),
        "w_x": np.asarray(inputs["L0_attn_a_w_x"], np.float32),
        "w_dt": np.asarray(inputs["L0_attn_a_w_dt"], np.float32),
        "b_dt": np.asarray(inputs["L0_attn_a_b_dt"], np.float32),
        "A_log": np.asarray(inputs["L0_attn_a_A_log"], np.float32),
        "Dp": np.asarray(inputs["L0_attn_a_Dp"], np.float32),
        "identin": _IDENT,
    }
    in_maps = [{**base, "x": xs[c]} for c in range(NCORE)]
    res = run_bass_kernel_spmd(p.nc, in_maps, core_ids=list(range(NCORE)))
    return res.results


_UT = np.triu(np.ones((128, 128))).astype(ml_dtypes.bfloat16)
_IOTA8 = np.tile(np.arange(8, dtype=np.float32), (128, 1))


def run_L2(inputs, l1res):
    p = _get("L2", build_L2)
    base = {
        "w_dt": np.asarray(inputs["L0_attn_a_w_dt"], np.float32),
        "b_dt": np.asarray(inputs["L0_attn_a_b_dt"], np.float32),
        "A_log": np.asarray(inputs["L0_attn_a_A_log"], np.float32),
        "wrin": np.asarray(inputs["L0_ffn_Wr"], np.float32),
        "brin": np.asarray(inputs["L0_ffn_br"], np.float32),
        "identin": _IDENT, "utincin": _UT, "iota8in": _IOTA8,
    }
    zero_h = np.zeros((128, 128), np.float32)
    in_maps = []
    for c in range(NCORE):
        m = dict(base)
        m["s1p"] = np.asarray(l1res[c]["s1p"])
        m["dtlt"] = np.asarray(l1res[c]["dtlt"])
        m["ctr"] = np.asarray(l1res[c]["ctr"])
        m["hrecv"] = np.asarray(l1res[c - 1]["hend"]) if (c & 1) else zero_h
        in_maps.append(m)
    res = run_bass_kernel_spmd(p.nc, in_maps, core_ids=list(range(NCORE)))
    return res.results


def run_L3(inputs, r2):
    p = _get("L3", build_L3)
    allcnt = np.stack([np.asarray(r2[c]["cnt_o"])[0] for c in range(NCORE)]).astype(np.float32)
    base = {
        "allcnt": allcnt,
        "w1": np.asarray(inputs["L0_ffn_W1"], np.float32),
        "b1": np.asarray(inputs["L0_ffn_b1"], np.float32),
        "w2": np.asarray(inputs["L0_ffn_W2"], np.float32),
        "b2": np.asarray(inputs["L0_ffn_b2"], np.float32),
        "winh": np.asarray(inputs["L0_attn_b_w_in"], np.float32),
        "binh": np.asarray(inputs["L0_attn_b_b_in"], np.float32),
        "wconv": np.asarray(inputs["L0_attn_b_w_conv"], np.float32),
        "bconv": np.asarray(inputs["L0_attn_b_b_conv"], np.float32),
        "identin": _IDENT, "iota8in": _IOTA8,
    }
    in_maps = []
    for c in range(NCORE):
        m = dict(base)
        m["s1"] = np.asarray(r2[c]["s1"])
        m["idxf_i"] = np.asarray(r2[c]["idxf_o"])
        m["gate_i"] = np.asarray(r2[c]["gate_o"])
        m["selpos_i"] = np.asarray(r2[c]["selpos_o"])
        cm = np.zeros((8, 1), np.float32)
        cm[:c] = 1.0
        m["cmask"] = cm
        in_maps.append(m)
    res = run_bass_kernel_spmd(p.nc, in_maps, core_ids=list(range(NCORE)))
    return res.results


_HGIDX = (127 + 128 * np.arange(9, dtype=np.int32)[None, :]
          - np.arange(128, dtype=np.int32)[:, None]).astype(np.int32)


def run_L4(inputs, r3):
    p = _get("L4", build_L4)
    _ = inputs["L0_attn_b_w1"]
    base = {
        "b1f": np.asarray(inputs["L0_attn_b_b1"], np.float32),
        "w2f": np.asarray(inputs["L0_attn_b_w2"], np.float32),
        "b2f": np.asarray(inputs["L0_attn_b_b2"], np.float32),
        "w3f": np.asarray(inputs["L0_attn_b_w3"], np.float32),
        "fb": np.asarray(inputs["L0_attn_b_filt_bias"], np.float32),
        "wo": np.asarray(inputs["L0_attn_b_w_out"], np.float32),
        "bo": np.asarray(inputs["L0_attn_b_b_out"], np.float32),
        "wconv": np.asarray(inputs["L0_attn_b_w_conv"], np.float32),
        "bconv": np.asarray(inputs["L0_attn_b_b_conv"], np.float32),
        "binh": np.asarray(inputs["L0_attn_b_b_in"], np.float32),
        "wrin": np.asarray(inputs["L0_ffn_Wr"], np.float32),
        "brin": np.asarray(inputs["L0_ffn_br"], np.float32),
        "identin": _IDENT, "utincin": _UT, "iota8in": _IOTA8, "hgidx": _HGIDX,
    }
    zvh = np.zeros((1024, D), ml_dtypes.bfloat16)
    zuh = np.zeros((128, 64), ml_dtypes.bfloat16)
    zbin = np.zeros((3072,), np.float32)
    in_maps = []
    for c in range(NCORE):
        odd = c & 1
        m = dict(base)
        m["s2"] = np.asarray(r3[c]["s2"])
        m["vf"] = np.asarray(r3[c]["vfull"])
        m["s0ct"] = np.asarray(r3[c]["s0ct"])
        m["uht"] = np.asarray(r3[c]["uht"])
        m["vhalo"] = (np.ascontiguousarray(np.asarray(r3[c - 1]["vfull"])[T - 1024:])
                      if odd else zvh)
        m["uhtp"] = np.asarray(r3[c - 1]["uht"]) if odd else zuh
        m["binsel"] = base["binh"] if odd else zbin
        in_maps.append(m)
    res = run_bass_kernel_spmd(p.nc, in_maps, core_ids=list(range(NCORE)))
    return res.results


def run_L5(inputs, r4):
    p = _get("L5", build_L5)
    allcnt = np.stack([np.asarray(r4[c]["cnt2_o"])[0] for c in range(NCORE)]).astype(np.float32)
    base = {
        "allcnt": allcnt,
        "w1": np.asarray(inputs["L0_ffn_W1"], np.float32),
        "b1": np.asarray(inputs["L0_ffn_b1"], np.float32),
        "w2": np.asarray(inputs["L0_ffn_W2"], np.float32),
        "b2": np.asarray(inputs["L0_ffn_b2"], np.float32),
        "identin": _IDENT, "iota8in": _IOTA8,
    }
    in_maps = []
    for c in range(NCORE):
        m = dict(base)
        m["s3"] = np.asarray(r4[c]["s3"])
        m["idxf_i"] = np.asarray(r4[c]["idxf2_o"])
        m["gate_i"] = np.asarray(r4[c]["gate2_o"])
        m["selpos_i"] = np.asarray(r4[c]["selpos2_o"])
        cm = np.zeros((8, 1), np.float32)
        cm[:c] = 1.0
        m["cmask"] = cm
        in_maps.append(m)
    res = run_bass_kernel_spmd(p.nc, in_maps, core_ids=list(range(NCORE)))
    return res.results


def kernel(**inputs):
    r1 = run_L1(inputs)
    r2 = run_L2(inputs, r1)
    r3 = run_L3(inputs, r2)
    r4 = run_L4(inputs, r3)
    r5 = run_L5(inputs, r4)
    out = np.zeros((4, 8192, 1024), np.float32)
    for c in range(NCORE):
        b, h = c >> 1, c & 1
        out[b, h * T:(h + 1) * T, :] = np.asarray(r5[c]["out"])
    return out
